# version 01
# baseline (speedup 1.0000x reference)
import numpy as np
import ml_dtypes
import concourse.bass as bass
import concourse.mybir as mybir
from concourse.bass_utils import run_bass_kernel_spmd

F32 = mybir.dt.float32
BF16 = mybir.dt.bfloat16
F16 = mybir.dt.float16
I32 = mybir.dt.int32
AF = mybir.ActivationFunctionType
ALU = mybir.AluOpType
AX = mybir.AxisListType
bf = ml_dtypes.bfloat16

B, S, H = 4, 4096, 4096
NH, KVH, D = 32, 2, 128
INTER = 11008
NCI = INTER // 128
CH = 512
SLOTS = [[0, 7, 3, 4], [1, 6, 2, 5]]
NU = [8, 32, 16, 24]
RSCALE = float(1.0 / np.sqrt(128.0))
EPS = 1e-5


def host_tables():
    inv = 1.0 / 10000.0 ** (np.arange(0, 64, 2, dtype=np.float64) / 64.0)
    pos = np.arange(S, dtype=np.float64)[:, None] * inv[None, :]
    cos_g = np.cos(pos).astype(np.float32)
    sin_g = np.sin(pos).astype(np.float32)
    ident = np.eye(128, dtype=bf)
    shifts = ((np.arange(128) % 8) * 4).astype(np.int32).reshape(128, 1)
    masks = []
    for half in (0, 1):
        m = np.zeros((4, 8, 128, 512), dtype=bf)
        for sl in range(4):
            g, nu = SLOTS[half][sl], NU[sl]
            nr = 4 * (g + 1)
            for j in range(8):
                kt = nu - 8 + j
                if kt >= nr:
                    continue
                kk = np.arange(128)[:, None]
                qq = np.arange(512)[None, :]
                m[sl, j] = ((g * CH + qq) >= (kt * 128 + kk)).astype(bf)
        masks.append(m)
    return cos_g, sin_g, ident, shifts, masks


class Sem:
    def __init__(self, nc, name):
        self.h = nc.alloc_semaphore(name=name)
        self.n = 0

    def inc(self, inst, k=1):
        inst.then_inc(self.h, k)
        self.n += k
        return self.n


class KB:
    def __init__(self, dbg=()):
        nc = bass.Bass("TRN2", target_bir_lowering=False)
        self.nc = nc
        self.dbg = set(dbg)
        self.dbg_t = {}
        self.sy, self.ve, self.sc, self.te, self.gp = \
            nc.sync, nc.vector, nc.scalar, nc.tensor, nc.gpsimd

        def inp(name, shape, dtype):
            return nc.dram_tensor(name, list(shape), dtype, kind="ExternalInput")
        self.i_xg = inp("x_glob", [S, H], F32)
        self.i_xo = inp("x_own", [S // 2, H], F32)
        self.i_cg = inp("cos_g", [S, 32], F32)
        self.i_sg = inp("sin_g", [S, 32], F32)
        self.i_co = inp("cos_o", [S // 2, 32], F32)
        self.i_so = inp("sin_o", [S // 2, 32], F32)
        self.i_msk = inp("masks", [4, 8, 128, 512], BF16)
        self.i_id = inp("ident", [128, 128], BF16)
        self.i_sh = inp("shifts", [128, 1], I32)
        self.i_qkvw = inp("qkv_w", [4608, H], F32)
        self.i_qkvb = inp("qkv_b", [1, 4608], F32)
        self.i_dw = inp("dense_w", [H, H], F32)
        self.i_gq = inp("gate_q", [512, INTER], I32)
        self.i_gs = inp("gate_s", [32, INTER], F16)
        self.i_gz = inp("gate_z", [32, INTER // 8], I32)
        self.i_uq = inp("up_q", [512, INTER], I32)
        self.i_us = inp("up_s", [32, INTER], F16)
        self.i_uz = inp("up_z", [32, INTER // 8], I32)
        self.i_dq = inp("down_q", [INTER // 8, H], I32)
        self.i_ds = inp("down_s", [NCI, H], F16)
        self.i_dz = inp("down_z", [NCI, H // 8], I32)
        self.o_y = nc.dram_tensor("y_own", [S // 2, H], F32, kind="ExternalOutput")

        D_ = lambda n, s, d: nc.dram_tensor(
            n, list(s), d,
            kind="ExternalOutput" if n in self.dbg else "Internal")
        self.d_qkvT = D_("d_qkvT", [9, H, 512], BF16)
        self.d_dT = D_("d_dT", [8, H, 512], BF16)
        self.d_K = D_("d_K", [128, 2 * S], BF16)
        self.d_V = D_("d_V", [128, 2 * S], BF16)
        self.d_Q = D_("d_Q", [NH, 4, 128, 512], BF16)
        self.d_x1 = D_("d_x1", [S // 2, H], F32)
        self.d_int = D_("d_int", [NCI, 128, S // 2], BF16)
        self.d_wd = D_("d_wd", [NCI, 128, H], BF16)
        self.d_zp = D_("d_zp", [NCI, H], BF16)
        self.d_nzp = D_("d_nzp", [2, 32, INTER], BF16)
        self.d_sg = D_("d_sg", [32, INTER], BF16)
        self.d_su = D_("d_su", [32, INTER], BF16)
        self.d_sd = D_("d_sd", [NCI, H], BF16)
        self.d_gqr = D_("d_gqr", [8 * 512, INTER], I32)
        self.d_uqr = D_("d_uqr", [8 * 512, INTER], I32)

        self.SY = Sem(nc, "sSY")
        self.GP = Sem(nc, "sGP")
        self.SC = Sem(nc, "sSC")
        self.VE = Sem(nc, "sVE")
        self.AC = Sem(nc, "sAC")
        self.PE = Sem(nc, "sPE")

        self.ident = nc.alloc_sbuf_tensor("sb_ident", [128, 128], BF16)
        self.ones = nc.alloc_sbuf_tensor("ones", [128, 128], BF16)
        self.shifts = nc.alloc_sbuf_tensor("sb_shifts", [128, 1], I32)
        self.statsB = nc.alloc_sbuf_tensor("statsB", [128, 16 * 48], F32)
        self.st48 = nc.alloc_sbuf_tensor("st48", [128, 48], F32)
        self.kmaxr = nc.alloc_sbuf_tensor("kmaxr", [128, 2], F32)
        self.smA = nc.alloc_sbuf_tensor("smA", [128, 16], F32)
        self.cosT = nc.alloc_sbuf_tensor("cosT", [128, 32], F32)
        self.sinT = nc.alloc_sbuf_tensor("sinT", [128, 32], F32)
        self.qb = nc.alloc_sbuf_tensor("qb", [1, 4608], BF16)
        self.Xg = nc.alloc_sbuf_tensor("Xg", [32, 2048], BF16)

        self.pp = [nc.alloc_psum_tensor(f"pp{i}", [128, 1024], F32) for i in range(2)]
        self.pb = [self.pp[0][:, 0:512], self.pp[0][:, 512:1024],
                   self.pp[1][:, 0:512], self.pp[1][:, 512:1024]]
        self._pbt = {}
        for i in range(4, 8):
            t = nc.alloc_psum_tensor(f"pb{i}", [128, 512], F32)
            self._pbt[i] = t
            self.pb.append(t[:, :])
        self.pbh = {i: self._pbt[i].bitcast(BF16) for i in (6, 7)}
        self.bank_free = {i: ("VE", 0) for i in range(8)}

    def v(self, emit):
        self.ve.wait_ge(self.VE.h, self.VE.n)
        return self.VE.inc(emit())

    def s(self, emit):
        self.sc.wait_ge(self.AC.h, self.AC.n)
        return self.AC.inc(emit())

    def t(self, emit):
        return self.PE.inc(emit())

    def sem(self, name):
        return {"SY": self.SY, "GP": self.GP, "SC": self.SC,
                "VE": self.VE, "AC": self.AC, "PE": self.PE}[name]

    def dma(self, q, out, in_):
        eng, sem = {"sy": (self.sy, self.SY), "gp": (self.gp, self.GP),
                    "sc": (self.sc, self.SC)}[q]
        return sem.inc(eng.dma_start(out=out, in_=in_), 16)

    def bank_wait(self, eng, bank):
        nm, cnt = self.bank_free[bank]
        if cnt > 0:
            eng.wait_ge(self.sem(nm).h, cnt)

    def bank_used(self, bank, sem_name, cnt):
        self.bank_free[bank] = (sem_name, cnt)

    def bar(self):
        self.nc.all_engine_barrier()
        for i in range(8):
            self.bank_free[i] = ("VE", 0)

    def dump(self, name, src_ap, shape, dtype):
        if name not in self.dbg:
            return
        if name not in self.dbg_t:
            self.dbg_t[name] = self.nc.dram_tensor(
                "dbg_" + name, list(shape), dtype, kind="ExternalOutput")
        idx = tuple(slice(None) for _ in shape)
        self.sy.wait_ge(self.VE.h, self.VE.n)
        self.sy.wait_ge(self.AC.h, self.AC.n)
        self.dma("sy", self.dbg_t[name][idx], src_ap)

    def dump_dram(self, name, dram_t, shape, dtype, via, n_parts=128):
        if name not in self.dbg:
            return
        t = self.nc.dram_tensor("dbg_" + name, list(shape), dtype,
                                kind="ExternalOutput")
        self.dbg_t[name] = t
        flat_src = dram_t.ap().flatten()
        flat_dst = t.ap().flatten()
        total = int(np.prod(shape))
        chunk = n_parts * 2048
        off = 0
        while off < total:
            n = min(chunk, total - off)
            rows = n // 2048 if n % 2048 == 0 else None
            if rows:
                w = self.dma("sy", via[0:rows, 0:2048],
                             flat_src[off:off + n].rearrange("(p f) -> p f", f=2048))
                self.gp.wait_ge(self.SY.h, w)
                self.dma("gp", flat_dst[off:off + n].rearrange("(p f) -> p f", f=2048),
                         via[0:rows, 0:2048])
            else:
                w = self.dma("sy", via[0:1, 0:n], flat_src[off:off + n][None, :])
                self.gp.wait_ge(self.SY.h, w)
                self.dma("gp", flat_dst[off:off + n][None, :], via[0:1, 0:n])
            off += n

    def p0_consts(self):
        self.dma("sy", self.ident[:], self.i_id[:, :])
        self.dma("sy", self.shifts[:], self.i_sh[:, :])
        self.dma("gp", self.qb[:], self.i_qkvb[:, :])
        self.v(lambda: self.ve.memset(self.ones[:], 1.0))

    def p0_qrep(self):
        for srci, dsti in ((self.i_gq, self.d_gqr), (self.i_uq, self.d_uqr)):
            for r0 in range(0, 512, 64):
                self.dma("sy",
                         dsti[r0 * 8:(r0 + 64) * 8, :]
                         .rearrange("(r j) n -> r j n", j=8),
                         srci[r0:r0 + 64, None, :].broadcast_to([64, 8, INTER]))

    def p0_scales(self):
        nc = self.nc
        with nc.sbuf_tensor("sbf", [128, INTER], BF16) as sbf:
            for src, dst, G, N in ((self.i_gs, self.d_sg, 32, INTER),
                                   (self.i_us, self.d_su, 32, INTER),
                                   (self.i_ds, self.d_sd, NCI, H)):
                self.gp.wait_ge(self.SY.h, self.SY.n)
                w = self.dma("gp", sbf[0:G, 0:N], src[:, :])
                self.sy.wait_ge(self.GP.h, w)
                self.dma("sy", dst[:, :], sbf[0:G, 0:N])
            self.gp.wait_ge(self.SY.h, self.SY.n)

    def p0_zeros(self):
        nc = self.nc
        with nc.sbuf_tensor("zpk", [128, INTER // 8], I32) as zpk, \
             nc.sbuf_tensor("zun", [128, INTER], I32) as zun, \
             nc.sbuf_tensor("zsc", [128, INTER], BF16) as zsc, \
             nc.sbuf_tensor("zot", [128, INTER], BF16) as zot:
            jobs = [(self.i_gz, self.i_gs, 32, INTER, -1.0, self.d_nzp[0]),
                    (self.i_uz, self.i_us, 32, INTER, -1.0, self.d_nzp[1]),
                    (self.i_dz, self.i_ds, NCI, H, 1.0, self.d_zp[:, :])]
            for z_in, s_in, G, N, sgn, dst in jobs:
                w1 = self.dma("gp", zpk[0:G, 0:N // 8], z_in[:, :])
                w2 = self.dma("gp", zsc[0:G, 0:N], s_in[:, :])
                self.ve.wait_ge(self.GP.h, max(w1, w2))
                for j in range(8):
                    self.v(lambda j=j, G=G, N=N: self.ve.tensor_scalar(
                        out=zun[0:G, 0:N][:, j::8], in0=zpk[0:G, 0:N // 8],
                        scalar1=j * 4, scalar2=15,
                        op0=ALU.logical_shift_right, op1=ALU.bitwise_and))
                self.v(lambda G=G, N=N: self.ve.tensor_scalar(
                    out=zun[0:G, 0:N], in0=zun[0:G, 0:N], scalar1=1, scalar2=None,
                    op0=ALU.add))
                n = self.v(lambda G=G, N=N, sgn=sgn: self.ve.scalar_tensor_tensor(
                    out=zot[0:G, 0:N], in0=zun[0:G, 0:N], scalar=sgn,
                    in1=zsc[0:G, 0:N], op0=ALU.mult, op1=ALU.mult))
                self.sy.wait_ge(self.VE.h, n)
                self.dma("sy", dst, zot[0:G, 0:N])
                self.gp.wait_ge(self.SY.h, self.SY.n)

    def p0_wT(self):
        nc = self.nc
        with nc.sbuf_tensor("wblk", [128, 2 * 4 * H], BF16) as wblk, \
             nc.sbuf_tensor("wst", [128, 2 * 512], BF16) as wst:
            wblk_free = [0, 0]
            for src, dst, nob in ((self.i_qkvw, self.d_qkvT, 9),
                                  (self.i_dw, self.d_dT, 8)):
                for ob in range(nob):
                    bsl = ob % 2
                    base = bsl * 4 * H
                    self.gp.wait_ge(self.PE.h, wblk_free[bsl])
                    for ot in range(4):
                        self.dma("gp", wblk[:, base + ot * H: base + (ot + 1) * H],
                                 src[ob * 512 + ot * 128: ob * 512 + (ot + 1) * 128, :])
                    w = self.GP.n
                    self.te.wait_ge(self.GP.h, w)
                    for hc in range(32):
                        bk = 6 + (hc % 2)
                        self.bank_wait(self.te, bk)
                        ins = None
                        for ot in range(4):
                            ins = self.te.transpose(
                                self.pbh[bk][:, ot * 128:(ot + 1) * 128],
                                wblk[:, base + ot * H + hc * 128:
                                     base + ot * H + (hc + 1) * 128],
                                self.ident[:])
                        pe_n = self.PE.inc(ins)
                        self.ve.wait_ge(self.PE.h, pe_n)
                        self.ve.wait_ge(self.SY.h, self.SY.n - 16)
                        n = self.v(lambda bk=bk, hc=hc: self.ve.tensor_copy(
                            wst[:, (hc % 2) * 512:(hc % 2) * 512 + 512],
                            self.pbh[bk][:, 0:512]))
                        self.bank_used(bk, "VE", n)
                        self.sy.wait_ge(self.VE.h, n)
                        self.dma("sy", dst[ob, hc * 128:(hc + 1) * 128, :],
                                 wst[:, (hc % 2) * 512:(hc % 2) * 512 + 512])
                    wblk_free[bsl] = self.PE.n

    def p0_down(self):
        nc = self.nc
        with nc.sbuf_tensor("qr", [128, 2 * H], I32) as qr, \
             nc.sbuf_tensor("qm", [128, H], I32) as qm, \
             nc.sbuf_tensor("srp", [128, 2 * H], BF16) as srp, \
             nc.sbuf_tensor("zrp", [128, 2 * H], BF16) as zrp, \
             nc.sbuf_tensor("tmpf", [128, H], F32) as tmpf, \
             nc.sbuf_tensor("wdo", [128, 2 * H], BF16) as wdo:
            dq = self.i_dq.rearrange("(c m) i -> m c i", m=16)
            ve_done = {}
            for c in range(NCI):
                sl = c % 2
                if c >= 2:
                    self.gp.wait_ge(self.VE.h, ve_done[c - 2])
                w1 = self.dma("gp", qr[:, sl * H:(sl + 1) * H],
                              dq[:, None, c:c + 1, :].broadcast_to([16, 8, 1, H]))
                w2 = self.dma("gp", srp[:, sl * H:(sl + 1) * H],
                              self.d_sd[None, c, :].broadcast_to([128, H]))
                w3 = self.dma("gp", zrp[:, sl * H:(sl + 1) * H],
                              self.d_zp[None, c, :].broadcast_to([128, H]))
                self.ve.wait_ge(self.GP.h, w3)
                self.ve.wait_ge(self.SY.h, max(0, self.SY.n - 16))
                self.v(lambda sl=sl: self.ve.tensor_scalar(
                    out=qm[:], in0=qr[:, sl * H:(sl + 1) * H], scalar1=self.shifts[:],
                    scalar2=15, op0=ALU.logical_shift_right, op1=ALU.bitwise_and))
                self.v(lambda sl=sl: self.ve.tensor_tensor(
                    out=tmpf[:], in0=qm[:], in1=srp[:, sl * H:(sl + 1) * H],
                    op=ALU.mult))
                n = self.v(lambda sl=sl: self.ve.tensor_tensor(
                    out=wdo[:, sl * H:(sl + 1) * H], in0=tmpf[:],
                    in1=zrp[:, sl * H:(sl + 1) * H], op=ALU.subtract))
                ve_done[c] = n
                self.sy.wait_ge(self.VE.h, n)
                self.dma("sy", self.d_wd[c, :, :], wdo[:, sl * H:(sl + 1) * H])

    def _ln_tile(self, xt_ap, out_bf_ap):
        for kb in range(8):
            self.v(lambda kb=kb: self.ve.bn_stats(
                self.st48[:, kb * 6:(kb + 1) * 6], xt_ap[:, kb * 512:(kb + 1) * 512]))
        self.v(lambda: self.ve.bn_aggr(self.smA[:, 0:2], self.st48[:]))
        self.v(lambda: self.ve.tensor_scalar(
            out=self.smA[:, 2:3], in0=self.smA[:, 1:2], scalar1=EPS, scalar2=None,
            op0=ALU.add))
        self.sc.wait_ge(self.VE.h, self.VE.n)
        self.s(lambda: self.sc.activation(
            out=self.smA[:, 3:4], in_=self.smA[:, 2:3], func=AF.Sqrt))
        self.ve.wait_ge(self.AC.h, self.AC.n)
        self.v(lambda: self.ve.reciprocal(self.smA[:, 4:5], self.smA[:, 3:4]))
        return self.v(lambda: self.ve.tensor_scalar(
            out=out_bf_ap, in0=xt_ap, scalar1=self.smA[:, 0:1],
            scalar2=self.smA[:, 4:5], op0=ALU.subtract, op1=ALU.mult))

    def _rope(self, st3, nheads):
        e = st3[:, :, 0:64:2]
        o = st3[:, :, 1:64:2]
        c = self.cosT[:, None, :].broadcast_to([128, nheads, 32])
        s_ = self.sinT[:, None, :].broadcast_to([128, nheads, 32])

        def sv(i):
            return self.ropeT[:, i * 32 * nheads:(i + 1) * 32 * nheads].rearrange(
                "p (h d) -> p h d", d=32)
        self.v(lambda: self.ve.tensor_tensor(out=sv(0), in0=e, in1=c, op=ALU.mult))
        self.v(lambda: self.ve.tensor_tensor(out=sv(1), in0=e, in1=s_, op=ALU.mult))
        self.v(lambda: self.ve.tensor_tensor(out=sv(2), in0=o, in1=s_, op=ALU.mult))
        self.v(lambda: self.ve.tensor_tensor(out=sv(3), in0=o, in1=c, op=ALU.mult))
        self.v(lambda: self.ve.tensor_tensor(out=e, in0=sv(0), in1=sv(2),
                                             op=ALU.subtract))
        return self.v(lambda: self.ve.tensor_tensor(out=o, in0=sv(3), in1=sv(1),
                                                    op=ALU.add))

    def p1(self):
        nc = self.nc
        with nc.sbuf_tensor("xt", [128, 2 * H], F32) as xt, \
             nc.sbuf_tensor("lntm", [128, 2 * H], BF16) as lntm, \
             nc.sbuf_tensor("lnfm", [128, 32 * 512], BF16) as lnfm, \
             nc.sbuf_tensor("slab", [128, 2 * 32 * 512], BF16) as slab, \
             nc.sbuf_tensor("stg", [128, 512], F32) as stg, \
             nc.sbuf_tensor("stgb", [128, 512], BF16) as stgb, \
             nc.sbuf_tensor("ropeTb", [128, 512], F32) as ropeTb, \
             nc.sbuf_tensor("ofm", [128, 512], BF16) as ofm:
            self.ropeT = ropeTb
            self.stg, self.stgb, self.ofm = stg, stgb, ofm
            slab_free = [0, 0]
            xt_free = [0, 0]
            lntm_free = [0, 0]
            for half in ("kv", "q"):
                nblocks = 8 if half == "kv" else 4
                for tb in range(nblocks):
                    xsrc = self.i_xg if half == "kv" else self.i_xo
                    ln_done = {}
                    for tt in range(4):
                        xs = tt % 2
                        self.gp.wait_ge(self.VE.h, xt_free[xs])
                        wx = self.dma("gp", xt[:, xs * H:(xs + 1) * H],
                                      xsrc[tb * 512 + tt * 128:
                                           tb * 512 + (tt + 1) * 128, :])
                        self.ve.wait_ge(self.GP.h, wx)
                        self.ve.wait_ge(self.PE.h, lntm_free[xs])
                        nln = self._ln_tile(xt[:, xs * H:(xs + 1) * H],
                                            lntm[:, xs * H:(xs + 1) * H])
                        xt_free[xs] = nln
                        self.te.wait_ge(self.VE.h, nln)
                        for hc in range(32):
                            bk = 6 + (hc % 2)
                            self.bank_wait(self.te, bk)
                            ins = self.te.transpose(
                                self.pbh[bk][:, 0:128],
                                lntm[:, xs * H + hc * 128: xs * H + (hc + 1) * 128],
                                self.ident[:])
                            pe_n = self.PE.inc(ins)
                            self.ve.wait_ge(self.PE.h, pe_n)
                            n = self.v(lambda bk=bk, hc=hc, tt=tt: self.ve.tensor_copy(
                                lnfm[:, hc * 512 + tt * 128: hc * 512 + tt * 128 + 128],
                                self.pbh[bk][:, 0:128]))
                            self.bank_used(bk, "VE", n)
                        lntm_free[xs] = self.PE.n
                        ln_done[tt] = self.VE.n
                    obs = [8] if half == "kv" else list(range(8))
                    for obi, ob in enumerate(obs):
                        ssl = obi % 2
                        self.sy.wait_ge(self.PE.h, slab_free[ssl])
                        ws = self.dma("sy", slab[:, ssl * 16384:(ssl + 1) * 16384]
                                      .rearrange("p (a b) -> p a b", b=512),
                                      self.d_qkvT[ob].rearrange("(a p) b -> p a b",
                                                                p=128))
                        self.te.wait_ge(self.SY.h, ws)
                        slab3 = slab[:, ssl * 16384:(ssl + 1) * 16384].rearrange(
                            "p (a b) -> p a b", b=512)
                        for tt in range(4):
                            pbk = self.pb[tt]
                            self.te.wait_ge(self.VE.h, ln_done[tt])
                            self.bank_wait(self.te, tt)
                            self.t(lambda pbk=pbk, ob=ob: self.te.matmul(
                                pbk, self.ones[0:1, :],
                                self.qb[0:1, ob * 512:(ob + 1) * 512],
                                start=True, stop=False))
                            for hc in range(32):
                                self.t(lambda pbk=pbk, hc=hc, tt=tt, slab3=slab3:
                                       self.te.matmul(
                                           pbk,
                                           lnfm[:, hc * 512 + tt * 128:
                                                hc * 512 + tt * 128 + 128],
                                           slab3[:, hc, :],
                                           start=False, stop=(hc == 31)))
                            self._p1_evac(half, tb, tt, ob, self.PE.n)
                        slab_free[ssl] = self.PE.n
            del self.ropeT

    def _p1_evac(self, half, tb, tt, ob, pe_mm):
        pbk = self.pb[tt]
        if half == "kv":
            csrc, ssrc, row0 = self.i_cg, self.i_sg, tb * 512 + tt * 128
        else:
            csrc, ssrc, row0 = self.i_co, self.i_so, tb * 512 + tt * 128
        self.sc.wait_ge(self.VE.h, self.VE.n)
        self.dma("sc", self.cosT[:], csrc[row0:row0 + 128, :])
        wc = self.dma("sc", self.sinT[:], ssrc[row0:row0 + 128, :])
        self.ve.wait_ge(self.PE.h, pe_mm)
        self.ve.wait_ge(self.SC.h, wc)
        self.ve.wait_ge(self.SY.h, self.SY.n)
        if half == "kv":
            self.v(lambda: self.ve.tensor_copy(self.stg[:, 0:256], pbk[:, 0:256]))
            self._rope(self.stg[:, 0:256].rearrange("p (h d) -> p h d", d=128), 2)
            self.v(lambda: self.ve.tensor_copy(self.stgb[:, 0:256], self.stg[:, 0:256]))
            nv = self.v(lambda: self.ve.tensor_copy(self.stgb[:, 256:512],
                                                    pbk[:, 256:512]))
            self.bank_used(tt, "VE", nv)
            sec = tb * 4 + tt
            self.sy.wait_ge(self.VE.h, nv)
            self.dma("sy", self.d_V[:, sec * 256:(sec + 1) * 256],
                     self.stgb[:, 256:512])
            self.te.wait_ge(self.VE.h, nv)
            ins = None
            self.bank_wait(self.te, 6)
            for kh in range(2):
                ins = self.te.transpose(self.pbh[6][:, kh * 128:(kh + 1) * 128],
                                        self.stgb[:, kh * 128:(kh + 1) * 128],
                                        self.ident[:])
            pe_n = self.PE.inc(ins)
            self.ve.wait_ge(self.PE.h, pe_n)
            nk = self.v(lambda: self.ve.tensor_copy(self.ofm[:, 0:256],
                                                    self.pbh[6][:, 0:256]))
            self.bank_used(6, "VE", nk)
            self.sy.wait_ge(self.VE.h, nk)
            for kh in range(2):
                self.dma("sy", self.d_K[:, kh * S + tb * 512 + tt * 128:
                                        kh * S + tb * 512 + (tt + 1) * 128],
                         self.ofm[:, kh * 128:(kh + 1) * 128])
        else:
            sl = tb
            self.v(lambda: self.ve.tensor_copy(self.stg[:], pbk))
            self._rope(self.stg[:].rearrange("p (h d) -> p h d", d=128), 4)
            nb = self.v(lambda: self.ve.tensor_copy(self.stgb[:], self.stg[:]))
            self.bank_used(tt, "VE", nb)
            self.te.wait_ge(self.VE.h, nb)
            ins = None
            self.bank_wait(self.te, 6)
            for ot in range(4):
                ins = self.te.transpose(self.pbh[6][:, ot * 128:(ot + 1) * 128],
                                        self.stgb[:, ot * 128:(ot + 1) * 128],
                                        self.ident[:])
            pe_n = self.PE.inc(ins)
            self.ve.wait_ge(self.PE.h, pe_n)
            nq = self.v(lambda: self.ve.tensor_copy(self.ofm[:], self.pbh[6][:, 0:512]))
            self.bank_used(6, "VE", nq)
            self.sy.wait_ge(self.VE.h, nq)
            for ot in range(4):
                head = ob * 4 + ot
                self.dma("sy", self.d_Q[head, sl, :, tt * 128:(tt + 1) * 128],
                         self.ofm[:, ot * 128:(ot + 1) * 128])

    def p3(self):
        nc = self.nc
        with nc.sbuf_tensor("Ksb", [128, 2 * S], BF16) as Ksb, \
             nc.sbuf_tensor("Vsb", [128, 2 * S], BF16) as Vsb, \
             nc.sbuf_tensor("ctx", [128, 32 * 1024], BF16) as ctx, \
             nc.sbuf_tensor("msl", [128, 8 * 512], BF16) as msl, \
             nc.sbuf_tensor("Qh", [128, 2 * 512], BF16) as Qh, \
             nc.sbuf_tensor("Eb", [128, 2 * 1024], BF16) as Eb, \
             nc.sbuf_tensor("qsq", [128, 512], BF16) as qsq, \
             nc.sbuf_tensor("recd", [128, 512], F32) as recd, \
             nc.sbuf_tensor("dslab", [128, 32 * 512], BF16) as dslab, \
             nc.sbuf_tensor("x1t", [128, 2 * 512], F32) as x1t:
            wk = self.dma("sy", Ksb[:], self.d_K[:, :])
            wv = self.dma("sy", Vsb[:], self.d_V[:, :])
            self.sc.wait_ge(self.SY.h, wv)
            self._p3_kmax(Ksb, qsq)
            qh_hist = [0, 0]
            for sh in range(2):
                for sli in range(2):
                    sl = sh * 2 + sli
                    self.sy.wait_ge(self.VE.h, self.VE.n)
                    wm = self.dma("sy", msl[:].rearrange("p (a b) -> p a b", b=512),
                                  self.i_msk[sl].rearrange("a p b -> p a b"))
                    self.ve.wait_ge(self.SY.h, wm)
                    nu = NU[sl]
                    for head in range(NH):
                        kvh = head // 16
                        qs = head % 2
                        self.gp.wait_ge(self.PE.h, qh_hist[-2])
                        wq = self.dma("gp", Qh[:, qs * 512:(qs + 1) * 512],
                                      self.d_Q[head, sl, :, :])
                        self._p3_head(sl, head, nu, kvh, qs, wq, Ksb, Vsb, ctx,
                                      msl, Qh, Eb, qsq, recd, sli)
                        qh_hist.append(self.PE.n)
                self._p3_dense(sh, ctx, dslab, x1t)
            self.dump("ctx", ctx[:], [128, 32 * 1024], BF16)

    def _p3_kmax(self, Ksb, qsq):
        for kvh in range(2):
            for kb in range(8):
                self.sc.wait_ge(self.PE.h, self.PE.n)
                ns = self.s(lambda kvh=kvh, kb=kb: self.sc.activation(
                    out=qsq[:], in_=Ksb[:, kvh * S + kb * 512:
                                       kvh * S + (kb + 1) * 512], func=AF.Square))
                self.te.wait_ge(self.AC.h, ns)
                self.bank_wait(self.te, 6)
                pe_n = self.t(lambda: self.te.matmul(
                    self.pb[6], self.ones[:], qsq[:], start=True, stop=True))
                self.ve.wait_ge(self.PE.h, pe_n)
                dst = self.smA[:, 8:9] if kb == 0 else self.smA[:, 9:10]
                n = self.v(lambda dst=dst: self.ve.reduce_max(
                    dst, self.pb[6], axis=AX.X))
                self.bank_used(6, "VE", n)
                if kb > 0:
                    self.v(lambda: self.ve.tensor_tensor(
                        out=self.smA[:, 8:9], in0=self.smA[:, 8:9],
                        in1=self.smA[:, 9:10], op=ALU.max))
            self.sc.wait_ge(self.VE.h, self.VE.n)
            ns = self.s(lambda: self.sc.activation(
                out=self.smA[:, 10:11], in_=self.smA[:, 8:9], func=AF.Sqrt))
            self.ve.wait_ge(self.AC.h, ns)
            self.v(lambda kvh=kvh: self.ve.tensor_scalar(
                out=self.kmaxr[:, kvh:kvh + 1], in0=self.smA[:, 10:11],
                scalar1=RSCALE, scalar2=None, op0=ALU.mult))

    def _p3_head(self, sl, head, nu, kvh, qs, wq, Ksb, Vsb, ctx, msl, Qh, Eb,
                 qsq, recd, sli):
        self.sc.wait_ge(self.GP.h, wq)
        self.sc.wait_ge(self.PE.h, self.PE.n)
        ns = self.s(lambda qs=qs: self.sc.activation(
            out=qsq[:], in_=Qh[:, qs * 512:(qs + 1) * 512], func=AF.Square))
        self.te.wait_ge(self.AC.h, ns)
        self.bank_wait(self.te, 6)
        pe_n = self.t(lambda: self.te.matmul(
            self.pb[6], self.ones[:], qsq[:], start=True, stop=True))
        self.ve.wait_ge(self.PE.h, pe_n)
        n = self.v(lambda: self.ve.reduce_max(self.smA[:, 11:12], self.pb[6],
                                              axis=AX.X))
        self.bank_used(6, "VE", n)
        self.sc.wait_ge(self.VE.h, self.VE.n)
        ns = self.s(lambda: self.sc.activation(
            out=self.smA[:, 12:13], in_=self.smA[:, 11:12], func=AF.Sqrt))
        self.ve.wait_ge(self.AC.h, ns)
        nbias = self.v(lambda kvh=kvh: self.ve.scalar_tensor_tensor(
            out=self.smA[:, 13:14], in0=self.smA[:, 12:13], scalar=-1.0,
            in1=self.kmaxr[:, kvh:kvh + 1], op0=ALU.mult, op1=ALU.mult))
        self.sc.wait_ge(self.VE.h, nbias)
        exp_done = {}
        pvm_done = {}
        for kp in range(nu // 2):
            pgrp = kp % 2
            ebs = (kp % 2) * 1024
            self.bank_wait(self.te, pgrp * 2)
            for kj in range(2):
                kt = kp * 2 + kj
                self.t(lambda pgrp=pgrp, kj=kj, kvh=kvh, kt=kt, qs=qs:
                       self.te.matmul(
                           self.pp[pgrp][:, kj * 512:(kj + 1) * 512],
                           Ksb[:, kvh * S + kt * 128: kvh * S + (kt + 1) * 128],
                           Qh[:, qs * 512:(qs + 1) * 512],
                           start=True, stop=True))
            pe_st = self.PE.n
            self.sc.wait_ge(self.PE.h, pe_st)
            if kp >= 2:
                self.sc.wait_ge(self.PE.h, pvm_done[kp - 2])
            ne = self.s(lambda pgrp=pgrp, ebs=ebs: self.sc.activation(
                out=Eb[:, ebs:ebs + 1024], in_=self.pp[pgrp][:],
                func=AF.Exp, bias=self.smA[:, 13:14], scale=RSCALE))
            exp_done[kp] = ne
            self.bank_used(pgrp * 2, "AC", ne)
            last_ve = None
            for kj in range(2):
                kt = kp * 2 + kj
                if kt >= nu - 8:
                    j = kt - (nu - 8)
                    self.ve.wait_ge(self.AC.h, ne)
                    last_ve = self.v(lambda ebs=ebs, kj=kj, j=j:
                                     self.ve.tensor_tensor(
                        out=Eb[:, ebs + kj * 512: ebs + (kj + 1) * 512],
                        in0=Eb[:, ebs + kj * 512: ebs + (kj + 1) * 512],
                        in1=msl[:, j * 512:(j + 1) * 512], op=ALU.mult))
            if last_ve is not None:
                self.te.wait_ge(self.VE.h, last_ve)
            else:
                self.te.wait_ge(self.AC.h, ne)
            if kp == 0:
                self.bank_wait(self.te, 4)
                self.bank_wait(self.te, 5)
            for kj in range(2):
                kt = kp * 2 + kj
                self.t(lambda ebs=ebs, kj=kj, kt=kt, nu=nu: self.te.matmul(
                    self.pb[4], self.ones[:],
                    Eb[:, ebs + kj * 512: ebs + (kj + 1) * 512],
                    start=(kt == 0), stop=(kt == nu - 1)))
                self.t(lambda ebs=ebs, kj=kj, kt=kt, kvh=kvh, nu=nu:
                       self.te.matmul(
                    self.pb[5],
                    Vsb[:, kt * 256 + kvh * 128: kt * 256 + (kvh + 1) * 128],
                    Eb[:, ebs + kj * 512: ebs + (kj + 1) * 512],
                    start=(kt == 0), stop=(kt == nu - 1)))
            pvm_done[kp] = self.PE.n
        self.ve.wait_ge(self.PE.h, self.PE.n)
        self.v(lambda: self.ve.reciprocal(recd[:], self.pb[4]))
        nctx = self.v(lambda head=head, sli=sli: self.ve.tensor_tensor(
            out=ctx[:, head * 1024 + sli * 512: head * 1024 + (sli + 1) * 512],
            in0=self.pb[5], in1=recd[:], op=ALU.mult))
        self.bank_used(4, "VE", nctx)
        self.bank_used(5, "VE", nctx)

    def _p3_dense(self, sh, ctx, dslab, x1t):
        dn_evac = {}
        for ob in range(8):
            self.sy.wait_ge(self.PE.h, self.PE.n)
            ws = self.dma("sy", dslab[:].rearrange("p (a b) -> p a b", b=512),
                          self.d_dT[ob].rearrange("(a p) b -> p a b", p=128))
            self.te.wait_ge(self.SY.h, ws)
            dsl3 = dslab[:].rearrange("p (a b) -> p a b", b=512)
            for tt in range(8):
                row = sh * 8 + tt
                bk = tt % 4
                pbk = self.pb[bk]
                xs = tt % 2
                self.gp.wait_ge(self.VE.h, dn_evac.get((xs,), 0))
                self.gp.wait_ge(self.SY.h, self.SY.n)
                wx = self.dma("gp", x1t[:, xs * 512:(xs + 1) * 512],
                              self.i_xo[row * 128:(row + 1) * 128,
                                        ob * 512:(ob + 1) * 512])
                self.bank_wait(self.te, bk)
                for dc in range(32):
                    self.t(lambda pbk=pbk, dc=dc, tt=tt, dsl3=dsl3: self.te.matmul(
                        pbk, ctx[:, dc * 1024 + tt * 128: dc * 1024 + tt * 128 + 128],
                        dsl3[:, dc, :], start=(dc == 0), stop=(dc == 31)))
                pe_n = self.PE.n
                self.ve.wait_ge(self.PE.h, pe_n)
                self.ve.wait_ge(self.GP.h, wx)
                self.v(lambda pbk=pbk, xs=xs: self.ve.tensor_tensor(
                    out=x1t[:, xs * 512:(xs + 1) * 512], in0=pbk,
                    in1=x1t[:, xs * 512:(xs + 1) * 512], op=ALU.add))
                nev = self.v(lambda row=row, ob=ob, xs=xs: self.ve.bn_stats(
                    self.statsB[:, row * 48 + ob * 6: row * 48 + (ob + 1) * 6],
                    x1t[:, xs * 512:(xs + 1) * 512]))
                self.bank_used(bk, "VE", nev)
                dn_evac[(xs,)] = nev
                self.sy.wait_ge(self.VE.h, nev)
                self.dma("sy", self.d_x1[row * 128:(row + 1) * 128,
                                         ob * 512:(ob + 1) * 512],
                         x1t[:, xs * 512:(xs + 1) * 512])

    def p45(self):
        nc = self.nc
        for th in range(2):
            with nc.sbuf_tensor(f"lnfm2_{th}", [128, 32 * 1024], BF16) as lnfm2:
                self._p4_half(th, lnfm2)
                self.bar()
                self._p5_half(th, lnfm2)
                self.bar()

    def _p4_half(self, th, lnfm2):
        nc = self.nc
        with nc.sbuf_tensor(f"x1b_{th}", [128, 2 * H], F32) as x1b, \
             nc.sbuf_tensor(f"ln2tm_{th}", [128, 2 * H], BF16) as ln2tm, \
             nc.sbuf_tensor(f"xgt_{th}", [128, 32], F32) as xgt, \
             nc.sbuf_tensor(f"xgtb_{th}", [128, 32], BF16) as xgtb:
            x1_free = [0, 0]
            ln2_free = [0, 0]
            for tt in range(8):
                row = th * 8 + tt
                xs = tt % 2
                self.gp.wait_ge(self.VE.h, x1_free[xs])
                wx = self.dma("gp", x1b[:, xs * H:(xs + 1) * H],
                              self.d_x1[row * 128:(row + 1) * 128, :])
                self.v(lambda row=row: self.ve.bn_aggr(
                    self.smA[:, 0:2], self.statsB[:, row * 48:(row + 1) * 48]))
                self.v(lambda: self.ve.tensor_scalar(
                    out=self.smA[:, 2:3], in0=self.smA[:, 1:2], scalar1=EPS,
                    scalar2=None, op0=ALU.add))
                self.sc.wait_ge(self.VE.h, self.VE.n)
                self.s(lambda: self.sc.activation(
                    out=self.smA[:, 3:4], in_=self.smA[:, 2:3], func=AF.Sqrt))
                self.ve.wait_ge(self.AC.h, self.AC.n)
                self.v(lambda: self.ve.reciprocal(self.smA[:, 4:5], self.smA[:, 3:4]))
                self.ve.wait_ge(self.GP.h, wx)
                self.ve.wait_ge(self.PE.h, ln2_free[xs])
                nl = self.v(lambda xs=xs: self.ve.tensor_scalar(
                    out=ln2tm[:, xs * H:(xs + 1) * H], in0=x1b[:, xs * H:(xs + 1) * H],
                    scalar1=self.smA[:, 0:1], scalar2=self.smA[:, 4:5],
                    op0=ALU.subtract, op1=ALU.mult))
                x1_free[xs] = nl
                nxg = self.v(lambda xs=xs: self.ve.tensor_reduce(
                    xgt[:, 0:32],
                    ln2tm[:, xs * H:(xs + 1) * H].rearrange("p (a b) -> p a b", b=128),
                    AX.X, ALU.add))
                nxg = self.v(lambda: self.ve.tensor_copy(xgtb[:, 0:32], xgt[:, 0:32]))
                self.te.wait_ge(self.VE.h, nxg)
                self.bank_wait(self.te, 6)
                ins = self.te.transpose(self.pbh[6][0:32, 0:128],
                                        xgtb[:, 0:32], self.ident[:])
                pe_n = self.PE.inc(ins)
                self.ve.wait_ge(self.PE.h, pe_n)
                n = self.v(lambda row=row: self.ve.tensor_copy(
                    self.Xg[0:32, row * 128 % 2048:(row * 128 % 2048) + 128],
                    self.pbh[6][0:32, 0:128]))
                self.bank_used(6, "VE", n)
                for hc in range(32):
                    bk = 6 + (hc % 2)
                    self.bank_wait(self.te, bk)
                    ins = self.te.transpose(
                        self.pbh[bk][:, 0:128],
                        ln2tm[:, xs * H + hc * 128: xs * H + (hc + 1) * 128],
                        self.ident[:])
                    pe_n = self.PE.inc(ins)
                    self.ve.wait_ge(self.PE.h, pe_n)
                    n = self.v(lambda bk=bk, hc=hc, tt=tt: self.ve.tensor_copy(
                        lnfm2[:, hc * 1024 + tt * 128: hc * 1024 + tt * 128 + 128],
                        self.pbh[bk][:, 0:128]))
                    self.bank_used(bk, "VE", n)
                ln2_free[xs] = self.PE.n

    def _p5_half(self, th, lnfm2):
        nc = self.nc
        with nc.sbuf_tensor(f"qrb_{th}", [128, 2 * 4096], I32) as qrb, \
             nc.sbuf_tensor(f"qmb_{th}", [128, 4096], I32) as qmb, \
             nc.sbuf_tensor(f"srb_{th}", [128, 2 * 4096], BF16) as srb, \
             nc.sbuf_tensor(f"wgb_{th}", [128, 2 * 4096], BF16) as wgb, \
             nc.sbuf_tensor(f"wub_{th}", [128, 2 * 4096], BF16) as wub, \
             nc.sbuf_tensor(f"nzb_{th}", [32, 2 * 256], BF16) as nzb, \
             nc.sbuf_tensor(f"silu_{th}", [128, 512], BF16) as silu, \
             nc.sbuf_tensor(f"intb_{th}", [128, 2 * 1024], BF16) as intb:
            gq3 = self.i_gq.rearrange("(c m) i -> m c i", m=16)
            uq3 = self.i_uq.rearrange("(c m) i -> m c i", m=16)
            st = {"qr": {}, "w": {}, "int": {}}
            for ic in range(NCI):
                self._p5_ic(th, ic, lnfm2, qrb, qmb, srb, wgb, wub, nzb, silu,
                            intb, gq3, uq3, st)

    def _p5_ic(self, th, ic, lnfm2, qrb, qmb, srb, wgb, wub, nzb, silu, intb,
               gq3, uq3, st):
        wsl = ic % 2
        par = 0 if ic % 2 == 0 else 4
        mats = ((0, self.d_gqr, self.d_sg, wgb), (1, self.d_uqr, self.d_su, wub))
        half_off = wsl * 4096
        for mi, q3, dsc, wdst in mats:
            self.gp.wait_ge(self.VE.h, st["qr"].get(wsl, 0))
            self.dma("gp",
                     qrb[:, half_off:half_off + 4096]
                     .rearrange("p (c i) -> p c i", i=128),
                     q3.rearrange("(c p) n -> p c n", p=128)
                     [:, :, ic * 128:(ic + 1) * 128])
            self.gp.wait_ge(self.VE.h, st.setdefault("sr", {}).get(wsl, 0))
            w2 = self.dma("gp",
                          srb[:, half_off:half_off + 4096]
                          .rearrange("p (c i) -> p c i", i=128),
                          dsc[None, :, ic * 128:(ic + 1) * 128]
                          .broadcast_to([128, 32, 128]))
            self.dma("sy", nzb[:, wsl * 256 + mi * 128: wsl * 256 + (mi + 1) * 128],
                     self.d_nzp[mi, :, ic * 128:(ic + 1) * 128])
            self.ve.wait_ge(self.GP.h, w2)
            self.v(lambda half_off=half_off: self.ve.tensor_scalar(
                out=qmb[:], in0=qrb[:, half_off:half_off + 4096],
                scalar1=self.shifts[:], scalar2=15,
                op0=ALU.logical_shift_right, op1=ALU.bitwise_and))
            st["qr"][wsl] = self.VE.n
            self.ve.wait_ge(self.PE.h, st["w"].get((wsl, mi), 0))
            nw = self.v(lambda wdst=wdst, half_off=half_off: self.ve.tensor_tensor(
                out=wdst[:, half_off:half_off + 4096], in0=qmb[:],
                in1=srb[:, half_off:half_off + 4096], op=ALU.mult))
            st["sr"][wsl] = nw
        self.te.wait_ge(self.VE.h, nw)
        self.te.wait_ge(self.SY.h, self.SY.n)
        for mi, _, _, wsrc in mats:
            for tc in range(2):
                bk = mi * 2 + tc + par
                self.bank_wait(self.te, bk)
                pbk = self.pb[bk]
                self.t(lambda pbk=pbk, mi=mi, tc=tc, wsl=wsl: self.te.matmul(
                    pbk, nzb[:, wsl * 256 + mi * 128: wsl * 256 + (mi + 1) * 128],
                    self.Xg[0:32, th * 1024 + tc * 512: th * 1024 + (tc + 1) * 512],
                    start=True, stop=False))
                for c in range(32):
                    self.t(lambda pbk=pbk, c=c, tc=tc, wsrc=wsrc, half_off=half_off:
                           self.te.matmul(
                        pbk, wsrc[:, half_off + c * 128: half_off + (c + 1) * 128],
                        lnfm2[:, c * 1024 + tc * 512: c * 1024 + (tc + 1) * 512],
                        start=False, stop=(c == 31)))
        st["w"][(wsl, 0)] = self.PE.n
        st["w"][(wsl, 1)] = self.PE.n
        pe_mm = self.PE.n
        if th == 0 and ic == 0 and "ffn0" in self.dbg:
            self.dump("lnfm2h0", lnfm2[:], [128, 32 * 1024], BF16)
            self.dump("wgb0", wgb[:, 0:4096], [128, 4096], BF16)
            self.dump("nzb0", nzb[0:32, 0:128], [32, 128], BF16)
            self.dump("Xg0", self.Xg[0:32, :], [32, 2048], BF16)
            with self.nc.sbuf_tensor("dbgps", [128, 2048], F32) as dbgps:
                self.ve.wait_ge(self.PE.h, pe_mm)
                for b in range(4):
                    self.v(lambda b=b: self.ve.tensor_copy(
                        dbgps[:, b * 512:(b + 1) * 512], self.pb[b]))
                self.dump("gups0", dbgps[:], [128, 2048], F32)
                self.bar()
        for tc in range(2):
            bkg = 0 + tc + par
            bku = 2 + tc + par
            self.sc.wait_ge(self.PE.h, pe_mm)
            self.sc.wait_ge(self.VE.h, self.VE.n)
            nsil = self.s(lambda bkg=bkg: self.sc.activation(
                out=silu[:], in_=self.pb[bkg], func=AF.Silu))
            self.ve.wait_ge(self.AC.h, nsil)
            self.ve.wait_ge(self.SY.h, st["int"].get((wsl, tc), 0))
            nint = self.v(lambda bku=bku, tc=tc, wsl=wsl: self.ve.tensor_tensor(
                out=intb[:, wsl * 1024 + tc * 512: wsl * 1024 + (tc + 1) * 512],
                in0=silu[:], in1=self.pb[bku], op=ALU.mult))
            self.bank_used(bkg, "AC", nsil)
            self.bank_used(bku, "VE", nint)
            self.sy.wait_ge(self.VE.h, nint)
            st["int"][(wsl, tc)] = self.dma(
                "sy", self.d_int[ic, :, th * 1024 + tc * 512:
                                 th * 1024 + (tc + 1) * 512],
                intb[:, wsl * 1024 + tc * 512: wsl * 1024 + (tc + 1) * 512])

    def p6(self):
        nc = self.nc
        with nc.sbuf_tensor("intf", [128, NCI * 512], BF16) as intf, \
             nc.sbuf_tensor("wdt", [128, 4 * 512], BF16) as wdt, \
             nc.sbuf_tensor("f16t", [128, 512], F16) as f16t, \
             nc.sbuf_tensor("x1r", [128, 2 * 512], F32) as x1r, \
             nc.sbuf_tensor("yot", [128, 2 * 512], F32) as yot:
            for tc in range(4):
                wi = 0
                self.sy.wait_ge(self.PE.h, self.PE.n)
                for ic in range(NCI):
                    wi = self.dma("sy", intf[:, ic * 512:(ic + 1) * 512],
                                  self.d_int[ic, :, tc * 512:(tc + 1) * 512])
                self.te.wait_ge(self.SY.h, wi)
                wd_free = {}
                ev_done = {}
                for hb in range(8):
                    par = hb % 2
                    for ic in range(NCI):
                        dsl = ic % 4
                        self.gp.wait_ge(self.PE.h, wd_free.get(dsl, 0))
                        wwd = self.dma("gp", wdt[:, dsl * 512:(dsl + 1) * 512],
                                       self.d_wd[ic, :, hb * 512:(hb + 1) * 512])
                        self.te.wait_ge(self.GP.h, wwd)
                        for tt in range(4):
                            bk = par * 4 + tt
                            if ic == 0:
                                self.bank_wait(self.te, bk)
                            self.t(lambda bk=bk, ic=ic, tt=tt, dsl=dsl:
                                   self.te.matmul(
                                self.pb[bk],
                                intf[:, ic * 512 + tt * 128: ic * 512 + tt * 128 + 128],
                                wdt[:, dsl * 512:(dsl + 1) * 512],
                                start=(ic == 0), stop=(ic == NCI - 1)))
                        wd_free[dsl] = self.PE.n
                    pe_mm = self.PE.n
                    for tt in range(4):
                        bk = par * 4 + tt
                        row = tc * 4 + tt
                        xs = tt % 2
                        self.gp.wait_ge(self.VE.h, ev_done.get(xs, 0))
                        wx = self.dma("gp", x1r[:, xs * 512:(xs + 1) * 512],
                                      self.d_x1[row * 128:(row + 1) * 128,
                                                hb * 512:(hb + 1) * 512])
                        self.ve.wait_ge(self.PE.h, pe_mm)
                        self.ve.wait_ge(self.GP.h, wx)
                        self.ve.wait_ge(self.SY.h, self.SY.n)
                        self.v(lambda bk=bk: self.ve.tensor_copy(f16t[:], self.pb[bk]))
                        nev = self.v(lambda xs=xs: self.ve.tensor_tensor(
                            out=yot[:, xs * 512:(xs + 1) * 512], in0=f16t[:],
                            in1=x1r[:, xs * 512:(xs + 1) * 512], op=ALU.add))
                        self.bank_used(bk, "VE", nev)
                        ev_done[xs] = nev
                        self.sy.wait_ge(self.VE.h, nev)
                        self.dma("sy", self.o_y[row * 128:(row + 1) * 128,
                                                hb * 512:(hb + 1) * 512],
                                 yot[:, xs * 512:(xs + 1) * 512])

    def build(self, upto=None):
        self.p0_consts()
        self.p0_qrep()
        self.p0_scales()
        self.p0_zeros()
        self.bar()
        self.p0_wT()
        self.bar()
        self.p0_down()
        self.bar()
        if upto == "p0":
            return self.finish()
        self.p1()
        self.bar()
        if upto == "p1":
            return self.finish()
        self.p3()
        self.bar()
        if upto == "p3":
            return self.finish()
        self.p45()
        self.bar()
        if upto == "p5":
            return self.finish()
        self.p6()
        return self.finish()

    def finish(self):
        self.bar()
        self.sy.wait_ge(self.SY.h, self.SY.n)
        self.gp.wait_ge(self.GP.h, self.GP.n)
        self.sc.wait_ge(self.SC.h, self.SC.n)
        return self.nc


def shard_inputs(inputs):
    cos_g, sin_g, ident, shifts, masks = host_tables()
    x = np.ascontiguousarray(inputs["L0_attn_hidden_states"])
    qkv_w = np.ascontiguousarray(inputs["L0_attn_qkv_w"])
    qkv_b = np.ascontiguousarray(inputs["L0_attn_qkv_b"]).reshape(1, 4608)
    dense_w = np.ascontiguousarray(inputs["L0_attn_dense_w"])
    in_maps = []
    for core in range(8):
        b, half = core // 2, core % 2
        rows = np.concatenate([np.arange(g * CH, (g + 1) * CH)
                               for g in SLOTS[half]])
        im = dict(
            x_glob=np.ascontiguousarray(x[b]),
            x_own=np.ascontiguousarray(x[b][rows]),
            cos_g=cos_g, sin_g=sin_g,
            cos_o=np.ascontiguousarray(cos_g[rows]),
            sin_o=np.ascontiguousarray(sin_g[rows]),
            masks=masks[half], ident=ident, shifts=shifts,
            qkv_w=qkv_w, qkv_b=qkv_b, dense_w=dense_w,
            gate_q=np.ascontiguousarray(inputs["L0_ffn_gate_qweight"]),
            gate_s=np.ascontiguousarray(inputs["L0_ffn_gate_scales"]),
            gate_z=np.ascontiguousarray(inputs["L0_ffn_gate_zeros"]),
            up_q=np.ascontiguousarray(inputs["L0_ffn_up_qweight"]),
            up_s=np.ascontiguousarray(inputs["L0_ffn_up_scales"]),
            up_z=np.ascontiguousarray(inputs["L0_ffn_up_zeros"]),
            down_q=np.ascontiguousarray(inputs["L0_ffn_down_qweight"]),
            down_s=np.ascontiguousarray(inputs["L0_ffn_down_scales"]),
            down_z=np.ascontiguousarray(inputs["L0_ffn_down_zeros"]),
        )
        in_maps.append(im)
    return in_maps


def kernel(**inputs):
    in_maps = shard_inputs(inputs)
    kb = KB()
    nc = kb.build()
    res = run_bass_kernel_spmd(nc, in_maps, core_ids=list(range(8)))
    out = np.zeros((B, S, H), dtype=np.float32)
    for core in range(8):
        b, half = core // 2, core % 2
        y = res.results[core]["y_own"]
        for si, g in enumerate(SLOTS[half]):
            out[b, g * CH:(g + 1) * CH] = y[si * CH:(si + 1) * CH]
    return out


# version 02
# speedup vs baseline: 1.0187x; 1.0187x over previous
import numpy as np
import ml_dtypes
import concourse.bass as bass
import concourse.mybir as mybir
from concourse.bass_utils import run_bass_kernel_spmd

F32 = mybir.dt.float32
BF16 = mybir.dt.bfloat16
F16 = mybir.dt.float16
I32 = mybir.dt.int32
AF = mybir.ActivationFunctionType
ALU = mybir.AluOpType
AX = mybir.AxisListType
bf = ml_dtypes.bfloat16

B, S, H = 4, 4096, 4096
NH, KVH, D = 32, 2, 128
INTER = 11008
NCI = INTER // 128
CH = 512
SLOTS = [[0, 7, 3, 4], [1, 6, 2, 5]]
NU = [8, 32, 16, 24]
RSCALE = float(1.0 / np.sqrt(128.0))
EPS = 1e-5


def host_tables():
    inv = 1.0 / 10000.0 ** (np.arange(0, 64, 2, dtype=np.float64) / 64.0)
    pos = np.arange(S, dtype=np.float64)[:, None] * inv[None, :]
    cos_g = np.cos(pos).astype(np.float32)
    sin_g = np.sin(pos).astype(np.float32)
    ident = np.eye(128, dtype=bf)
    shifts = ((np.arange(128) % 8) * 4).astype(np.int32).reshape(128, 1)
    masks = []
    for half in (0, 1):
        m = np.zeros((4, 8, 128, 512), dtype=bf)
        for sl in range(4):
            g, nu = SLOTS[half][sl], NU[sl]
            nr = 4 * (g + 1)
            for j in range(8):
                kt = nu - 8 + j
                if kt >= nr:
                    continue
                kk = np.arange(128)[:, None]
                qq = np.arange(512)[None, :]
                m[sl, j] = ((g * CH + qq) >= (kt * 128 + kk)).astype(bf)
        masks.append(m)
    return cos_g, sin_g, ident, shifts, masks


class Sem:
    def __init__(self, nc, name):
        self.h = nc.alloc_semaphore(name=name)
        self.n = 0

    def inc(self, inst, k=1):
        inst.then_inc(self.h, k)
        self.n += k
        return self.n


class KB:
    def __init__(self, dbg=()):
        nc = bass.Bass("TRN2", target_bir_lowering=False)
        self.nc = nc
        self.dbg = set(dbg)
        self.dbg_t = {}
        self.sy, self.ve, self.sc, self.te, self.gp = \
            nc.sync, nc.vector, nc.scalar, nc.tensor, nc.gpsimd

        def inp(name, shape, dtype):
            return nc.dram_tensor(name, list(shape), dtype, kind="ExternalInput")
        self.i_xg = inp("x_glob", [S, H], F32)
        self.i_xo = inp("x_own", [S // 2, H], F32)
        self.i_cg = inp("cos_g", [S, 32], F32)
        self.i_sg = inp("sin_g", [S, 32], F32)
        self.i_co = inp("cos_o", [S // 2, 32], F32)
        self.i_so = inp("sin_o", [S // 2, 32], F32)
        self.i_msk = inp("masks", [4, 8, 128, 512], BF16)
        self.i_id = inp("ident", [128, 128], BF16)
        self.i_sh = inp("shifts", [128, 1], I32)
        self.i_qkvw = inp("qkv_w", [4608, H], F32)
        self.i_qkvb = inp("qkv_b", [1, 4608], F32)
        self.i_dw = inp("dense_w", [H, H], F32)
        self.i_gq = inp("gate_q", [512, INTER], I32)
        self.i_gs = inp("gate_s", [32, INTER], F16)
        self.i_gz = inp("gate_z", [32, INTER // 8], I32)
        self.i_uq = inp("up_q", [512, INTER], I32)
        self.i_us = inp("up_s", [32, INTER], F16)
        self.i_uz = inp("up_z", [32, INTER // 8], I32)
        self.i_dq = inp("down_q", [INTER // 8, H], I32)
        self.i_ds = inp("down_s", [NCI, H], F16)
        self.i_dz = inp("down_z", [NCI, H // 8], I32)
        self.o_y = nc.dram_tensor("y_own", [S // 2, H], F32, kind="ExternalOutput")

        D_ = lambda n, s, d: nc.dram_tensor(
            n, list(s), d,
            kind="ExternalOutput" if n in self.dbg else "Internal")
        self.d_qkvT = D_("d_qkvT", [9, H, 512], BF16)
        self.d_dT = D_("d_dT", [8, H, 512], BF16)
        self.d_K = D_("d_K", [128, 2 * S], BF16)
        self.d_V = D_("d_V", [128, 2 * S], BF16)
        self.d_Q = D_("d_Q", [NH, 4, 128, 512], BF16)
        self.d_x1 = D_("d_x1", [S // 2, H], F32)
        self.d_int = D_("d_int", [NCI, 128, S // 2], BF16)
        self.d_wd = D_("d_wd", [NCI, 128, H], BF16)
        self.d_zp = D_("d_zp", [NCI, H], BF16)
        self.d_nzp = D_("d_nzp", [2, 32, INTER], BF16)
        self.d_sg = D_("d_sg", [32, INTER], BF16)
        self.d_su = D_("d_su", [32, INTER], BF16)
        self.d_sd = D_("d_sd", [NCI, H], BF16)
        self.d_gqr = D_("d_gqr", [8 * 512, INTER], I32)
        self.d_uqr = D_("d_uqr", [8 * 512, INTER], I32)
        self.d_gqr2 = D_("d_gqr2", [NCI, 128, 32 * 128], I32)
        self.d_uqr2 = D_("d_uqr2", [NCI, 128, 32 * 128], I32)
        self.d_sgt = D_("d_sgt", [NCI, 32 * 128], BF16)
        self.d_sut = D_("d_sut", [NCI, 32 * 128], BF16)

        self.SY = Sem(nc, "sSY")
        self.GP = Sem(nc, "sGP")
        self.SC = Sem(nc, "sSC")
        self.VE = Sem(nc, "sVE")
        self.AC = Sem(nc, "sAC")
        self.PE = Sem(nc, "sPE")

        self.ident = nc.alloc_sbuf_tensor("sb_ident", [128, 128], BF16)
        self.ones = nc.alloc_sbuf_tensor("ones", [128, 128], BF16)
        self.shifts = nc.alloc_sbuf_tensor("sb_shifts", [128, 1], I32)
        self.statsB = nc.alloc_sbuf_tensor("statsB", [128, 16 * 48], F32)
        self.st48 = nc.alloc_sbuf_tensor("st48", [128, 48], F32)
        self.kmaxr = nc.alloc_sbuf_tensor("kmaxr", [128, 2], F32)
        self.smA = nc.alloc_sbuf_tensor("smA", [128, 16], F32)
        self.cosT = nc.alloc_sbuf_tensor("cosT", [128, 32], F32)
        self.sinT = nc.alloc_sbuf_tensor("sinT", [128, 32], F32)
        self.qb = nc.alloc_sbuf_tensor("qb", [1, 4608], BF16)
        self.Xg = nc.alloc_sbuf_tensor("Xg", [32, 2048], BF16)

        self.pp = [nc.alloc_psum_tensor(f"pp{i}", [128, 1024], F32) for i in range(2)]
        self.pb = [self.pp[0][:, 0:512], self.pp[0][:, 512:1024],
                   self.pp[1][:, 0:512], self.pp[1][:, 512:1024]]
        self._pbt = {}
        for i in range(4, 8):
            t = nc.alloc_psum_tensor(f"pb{i}", [128, 512], F32)
            self._pbt[i] = t
            self.pb.append(t[:, :])
        self.pbh = {i: self._pbt[i].bitcast(BF16) for i in (6, 7)}
        self.bank_free = {i: ("VE", 0) for i in range(8)}

    def v(self, emit):
        self.ve.wait_ge(self.VE.h, self.VE.n)
        return self.VE.inc(emit())

    def s(self, emit):
        self.sc.wait_ge(self.AC.h, self.AC.n)
        return self.AC.inc(emit())

    def t(self, emit):
        return self.PE.inc(emit())

    def sem(self, name):
        return {"SY": self.SY, "GP": self.GP, "SC": self.SC,
                "VE": self.VE, "AC": self.AC, "PE": self.PE}[name]

    def dma(self, q, out, in_):
        eng, sem = {"sy": (self.sy, self.SY), "gp": (self.gp, self.GP),
                    "sc": (self.sc, self.SC)}[q]
        return sem.inc(eng.dma_start(out=out, in_=in_), 16)

    def bank_wait(self, eng, bank):
        nm, cnt = self.bank_free[bank]
        if cnt > 0:
            eng.wait_ge(self.sem(nm).h, cnt)

    def bank_used(self, bank, sem_name, cnt):
        self.bank_free[bank] = (sem_name, cnt)

    def bar(self):
        self.nc.all_engine_barrier()
        for i in range(8):
            self.bank_free[i] = ("VE", 0)

    def dump(self, name, src_ap, shape, dtype):
        if name not in self.dbg:
            return
        if name not in self.dbg_t:
            self.dbg_t[name] = self.nc.dram_tensor(
                "dbg_" + name, list(shape), dtype, kind="ExternalOutput")
        idx = tuple(slice(None) for _ in shape)
        self.sy.wait_ge(self.VE.h, self.VE.n)
        self.sy.wait_ge(self.AC.h, self.AC.n)
        self.dma("sy", self.dbg_t[name][idx], src_ap)

    def dump_dram(self, name, dram_t, shape, dtype, via, n_parts=128):
        if name not in self.dbg:
            return
        t = self.nc.dram_tensor("dbg_" + name, list(shape), dtype,
                                kind="ExternalOutput")
        self.dbg_t[name] = t
        flat_src = dram_t.ap().flatten()
        flat_dst = t.ap().flatten()
        total = int(np.prod(shape))
        chunk = n_parts * 2048
        off = 0
        while off < total:
            n = min(chunk, total - off)
            rows = n // 2048 if n % 2048 == 0 else None
            if rows:
                w = self.dma("sy", via[0:rows, 0:2048],
                             flat_src[off:off + n].rearrange("(p f) -> p f", f=2048))
                self.gp.wait_ge(self.SY.h, w)
                self.dma("gp", flat_dst[off:off + n].rearrange("(p f) -> p f", f=2048),
                         via[0:rows, 0:2048])
            else:
                w = self.dma("sy", via[0:1, 0:n], flat_src[off:off + n][None, :])
                self.gp.wait_ge(self.SY.h, w)
                self.dma("gp", flat_dst[off:off + n][None, :], via[0:1, 0:n])
            off += n

    def p0_consts(self):
        self.dma("sy", self.ident[:], self.i_id[:, :])
        self.dma("sy", self.shifts[:], self.i_sh[:, :])
        self.dma("gp", self.qb[:], self.i_qkvb[:, :])
        self.v(lambda: self.ve.memset(self.ones[:], 1.0))

    def p0_qrep(self):
        for srci, dsti in ((self.i_gq, self.d_gqr), (self.i_uq, self.d_uqr)):
            for r0 in range(0, 512, 64):
                self.dma("sc",
                         dsti[r0 * 8:(r0 + 64) * 8, :]
                         .rearrange("(r j) n -> r j n", j=8),
                         srci[r0:r0 + 64, None, :].broadcast_to([64, 8, INTER]))
        qrep_done = self.SC.n
        self.sc.wait_ge(self.SC.h, qrep_done)
        for srci, dsti in ((self.d_gqr, self.d_gqr2), (self.d_uqr, self.d_uqr2)):
            s3 = srci.rearrange("(c p) n -> p c n", p=128)
            for ic in range(NCI):
                self.dma("sc", dsti[ic, :, :].rearrange("p (c i) -> p c i", i=128),
                         s3[:, :, ic * 128:(ic + 1) * 128])
        self.qtile_done = self.SC.n

    def p0_scales(self):
        nc = self.nc
        with nc.sbuf_tensor("sbf", [128, INTER], BF16) as sbf:
            for src, dst, G, N in ((self.i_gs, self.d_sg, 32, INTER),
                                   (self.i_us, self.d_su, 32, INTER),
                                   (self.i_ds, self.d_sd, NCI, H)):
                self.gp.wait_ge(self.SY.h, self.SY.n)
                w = self.dma("gp", sbf[0:G, 0:N], src[:, :])
                self.sy.wait_ge(self.GP.h, w)
                self.dma("sy", dst[:, :], sbf[0:G, 0:N])
            self.gp.wait_ge(self.SY.h, self.SY.n)
        self.sc.wait_ge(self.SY.h, self.SY.n)
        self.dma("sc", self.d_sgt[:, :].rearrange("ic (c i) -> ic c i", i=128),
                 self.d_sg.rearrange("c (ic i) -> ic c i", i=128))
        self.dma("sc", self.d_sut[:, :].rearrange("ic (c i) -> ic c i", i=128),
                 self.d_su.rearrange("c (ic i) -> ic c i", i=128))
        self.qtile_done2 = self.SC.n

    def p0_zeros(self):
        nc = self.nc
        with nc.sbuf_tensor("zpk", [128, INTER // 8], I32) as zpk, \
             nc.sbuf_tensor("zun", [128, INTER], I32) as zun, \
             nc.sbuf_tensor("zsc", [128, INTER], BF16) as zsc, \
             nc.sbuf_tensor("zot", [128, INTER], BF16) as zot:
            jobs = [(self.i_gz, self.i_gs, 32, INTER, -1.0, self.d_nzp[0]),
                    (self.i_uz, self.i_us, 32, INTER, -1.0, self.d_nzp[1]),
                    (self.i_dz, self.i_ds, NCI, H, 1.0, self.d_zp[:, :])]
            for z_in, s_in, G, N, sgn, dst in jobs:
                w1 = self.dma("gp", zpk[0:G, 0:N // 8], z_in[:, :])
                w2 = self.dma("gp", zsc[0:G, 0:N], s_in[:, :])
                self.ve.wait_ge(self.GP.h, max(w1, w2))
                for j in range(8):
                    self.v(lambda j=j, G=G, N=N: self.ve.tensor_scalar(
                        out=zun[0:G, 0:N][:, j::8], in0=zpk[0:G, 0:N // 8],
                        scalar1=j * 4, scalar2=15,
                        op0=ALU.logical_shift_right, op1=ALU.bitwise_and))
                self.v(lambda G=G, N=N: self.ve.tensor_scalar(
                    out=zun[0:G, 0:N], in0=zun[0:G, 0:N], scalar1=1, scalar2=None,
                    op0=ALU.add))
                n = self.v(lambda G=G, N=N, sgn=sgn: self.ve.scalar_tensor_tensor(
                    out=zot[0:G, 0:N], in0=zun[0:G, 0:N], scalar=sgn,
                    in1=zsc[0:G, 0:N], op0=ALU.mult, op1=ALU.mult))
                self.sy.wait_ge(self.VE.h, n)
                self.dma("sy", dst, zot[0:G, 0:N])
                self.gp.wait_ge(self.SY.h, self.SY.n)

    def p0_wT(self):
        nc = self.nc
        with nc.sbuf_tensor("wblk", [128, 2 * 4 * H], BF16) as wblk, \
             nc.sbuf_tensor("wst", [128, 2 * 512], BF16) as wst:
            wblk_free = [0, 0]
            for src, dst, nob in ((self.i_qkvw, self.d_qkvT, 9),
                                  (self.i_dw, self.d_dT, 8)):
                for ob in range(nob):
                    bsl = ob % 2
                    base = bsl * 4 * H
                    self.gp.wait_ge(self.PE.h, wblk_free[bsl])
                    for ot in range(4):
                        self.dma("gp", wblk[:, base + ot * H: base + (ot + 1) * H],
                                 src[ob * 512 + ot * 128: ob * 512 + (ot + 1) * 128, :])
                    w = self.GP.n
                    self.te.wait_ge(self.GP.h, w)
                    for hc in range(32):
                        bk = 6 + (hc % 2)
                        self.bank_wait(self.te, bk)
                        ins = None
                        for ot in range(4):
                            ins = self.te.transpose(
                                self.pbh[bk][:, ot * 128:(ot + 1) * 128],
                                wblk[:, base + ot * H + hc * 128:
                                     base + ot * H + (hc + 1) * 128],
                                self.ident[:])
                        pe_n = self.PE.inc(ins)
                        self.ve.wait_ge(self.PE.h, pe_n)
                        self.ve.wait_ge(self.SY.h, self.SY.n - 16)
                        n = self.v(lambda bk=bk, hc=hc: self.ve.tensor_copy(
                            wst[:, (hc % 2) * 512:(hc % 2) * 512 + 512],
                            self.pbh[bk][:, 0:512]))
                        self.bank_used(bk, "VE", n)
                        self.sy.wait_ge(self.VE.h, n)
                        self.dma("sy", dst[ob, hc * 128:(hc + 1) * 128, :],
                                 wst[:, (hc % 2) * 512:(hc % 2) * 512 + 512])
                    wblk_free[bsl] = self.PE.n

    def p0_down(self):
        nc = self.nc
        with nc.sbuf_tensor("qr", [128, 2 * H], I32) as qr, \
             nc.sbuf_tensor("qm", [128, H], I32) as qm, \
             nc.sbuf_tensor("srp", [128, 2 * H], BF16) as srp, \
             nc.sbuf_tensor("zrp", [128, 2 * H], BF16) as zrp, \
             nc.sbuf_tensor("tmpf", [128, H], F32) as tmpf, \
             nc.sbuf_tensor("wdo", [128, 2 * H], BF16) as wdo:
            dq = self.i_dq.rearrange("(c m) i -> m c i", m=16)
            ve_done = {}
            for c in range(NCI):
                sl = c % 2
                if c >= 2:
                    self.gp.wait_ge(self.VE.h, ve_done[c - 2])
                w1 = self.dma("gp", qr[:, sl * H:(sl + 1) * H],
                              dq[:, None, c:c + 1, :].broadcast_to([16, 8, 1, H]))
                w2 = self.dma("gp", srp[:, sl * H:(sl + 1) * H],
                              self.d_sd[None, c, :].broadcast_to([128, H]))
                w3 = self.dma("gp", zrp[:, sl * H:(sl + 1) * H],
                              self.d_zp[None, c, :].broadcast_to([128, H]))
                self.ve.wait_ge(self.GP.h, w3)
                self.ve.wait_ge(self.SY.h, max(0, self.SY.n - 16))
                self.v(lambda sl=sl: self.ve.tensor_scalar(
                    out=qm[:], in0=qr[:, sl * H:(sl + 1) * H], scalar1=self.shifts[:],
                    scalar2=15, op0=ALU.logical_shift_right, op1=ALU.bitwise_and))
                self.v(lambda sl=sl: self.ve.tensor_tensor(
                    out=tmpf[:], in0=qm[:], in1=srp[:, sl * H:(sl + 1) * H],
                    op=ALU.mult))
                n = self.v(lambda sl=sl: self.ve.tensor_tensor(
                    out=wdo[:, sl * H:(sl + 1) * H], in0=tmpf[:],
                    in1=zrp[:, sl * H:(sl + 1) * H], op=ALU.subtract))
                ve_done[c] = n
                self.sy.wait_ge(self.VE.h, n)
                self.dma("sy", self.d_wd[c, :, :], wdo[:, sl * H:(sl + 1) * H])

    def _ln_tile(self, xt_ap, out_bf_ap):
        for kb in range(8):
            self.v(lambda kb=kb: self.ve.bn_stats(
                self.st48[:, kb * 6:(kb + 1) * 6], xt_ap[:, kb * 512:(kb + 1) * 512]))
        self.v(lambda: self.ve.bn_aggr(self.smA[:, 0:2], self.st48[:]))
        self.v(lambda: self.ve.tensor_scalar(
            out=self.smA[:, 2:3], in0=self.smA[:, 1:2], scalar1=EPS, scalar2=None,
            op0=ALU.add))
        self.sc.wait_ge(self.VE.h, self.VE.n)
        self.s(lambda: self.sc.activation(
            out=self.smA[:, 3:4], in_=self.smA[:, 2:3], func=AF.Sqrt))
        self.ve.wait_ge(self.AC.h, self.AC.n)
        self.v(lambda: self.ve.reciprocal(self.smA[:, 4:5], self.smA[:, 3:4]))
        return self.v(lambda: self.ve.tensor_scalar(
            out=out_bf_ap, in0=xt_ap, scalar1=self.smA[:, 0:1],
            scalar2=self.smA[:, 4:5], op0=ALU.subtract, op1=ALU.mult))

    def _rope(self, st3, nheads):
        e = st3[:, :, 0:64:2]
        o = st3[:, :, 1:64:2]
        c = self.cosT[:, None, :].broadcast_to([128, nheads, 32])
        s_ = self.sinT[:, None, :].broadcast_to([128, nheads, 32])

        def sv(i):
            return self.ropeT[:, i * 32 * nheads:(i + 1) * 32 * nheads].rearrange(
                "p (h d) -> p h d", d=32)
        self.v(lambda: self.ve.tensor_tensor(out=sv(0), in0=e, in1=c, op=ALU.mult))
        self.v(lambda: self.ve.tensor_tensor(out=sv(1), in0=e, in1=s_, op=ALU.mult))
        self.v(lambda: self.ve.tensor_tensor(out=sv(2), in0=o, in1=s_, op=ALU.mult))
        self.v(lambda: self.ve.tensor_tensor(out=sv(3), in0=o, in1=c, op=ALU.mult))
        self.v(lambda: self.ve.tensor_tensor(out=e, in0=sv(0), in1=sv(2),
                                             op=ALU.subtract))
        return self.v(lambda: self.ve.tensor_tensor(out=o, in0=sv(3), in1=sv(1),
                                                    op=ALU.add))

    def p1(self):
        nc = self.nc
        with nc.sbuf_tensor("xt", [128, 2 * H], F32) as xt, \
             nc.sbuf_tensor("lntm", [128, 2 * H], BF16) as lntm, \
             nc.sbuf_tensor("lnfm", [128, 32 * 512], BF16) as lnfm, \
             nc.sbuf_tensor("slab", [128, 2 * 32 * 512], BF16) as slab, \
             nc.sbuf_tensor("stg", [128, 512], F32) as stg, \
             nc.sbuf_tensor("stgb", [128, 512], BF16) as stgb, \
             nc.sbuf_tensor("ropeTb", [128, 512], F32) as ropeTb, \
             nc.sbuf_tensor("ofm", [128, 512], BF16) as ofm:
            self.ropeT = ropeTb
            self.stg, self.stgb, self.ofm = stg, stgb, ofm
            slab_free = [0, 0]
            xt_free = [0, 0]
            lntm_free = [0, 0]
            for half in ("kv", "q"):
                nblocks = 8 if half == "kv" else 4
                for tb in range(nblocks):
                    xsrc = self.i_xg if half == "kv" else self.i_xo
                    ln_done = {}
                    for tt in range(4):
                        xs = tt % 2
                        self.gp.wait_ge(self.VE.h, xt_free[xs])
                        wx = self.dma("gp", xt[:, xs * H:(xs + 1) * H],
                                      xsrc[tb * 512 + tt * 128:
                                           tb * 512 + (tt + 1) * 128, :])
                        self.ve.wait_ge(self.GP.h, wx)
                        self.ve.wait_ge(self.PE.h, lntm_free[xs])
                        nln = self._ln_tile(xt[:, xs * H:(xs + 1) * H],
                                            lntm[:, xs * H:(xs + 1) * H])
                        xt_free[xs] = nln
                        self.te.wait_ge(self.VE.h, nln)
                        lnfm3 = lnfm[:].rearrange("p (hc t) -> p hc t", t=512)
                        for hg in range(8):
                            bk = 6 + (hg % 2)
                            self.bank_wait(self.te, bk)
                            for hj in range(4):
                                hc = hg * 4 + hj
                                ins = self.te.transpose(
                                    self.pbh[bk][:, hj * 128:(hj + 1) * 128],
                                    lntm[:, xs * H + hc * 128: xs * H + (hc + 1) * 128],
                                    self.ident[:])
                            pe_n = self.PE.inc(ins)
                            self.ve.wait_ge(self.PE.h, pe_n)
                            n = self.v(lambda bk=bk, hg=hg, tt=tt, lnfm3=lnfm3:
                                       self.ve.tensor_copy(
                                lnfm3[:, hg * 4:(hg + 1) * 4, tt * 128:(tt + 1) * 128],
                                self.pbh[bk][:, 0:512].rearrange(
                                    "p (a b) -> p a b", b=128)))
                            self.bank_used(bk, "VE", n)
                        lntm_free[xs] = self.PE.n
                        ln_done[tt] = self.VE.n
                    obs = [8] if half == "kv" else list(range(8))
                    for obi, ob in enumerate(obs):
                        ssl = obi % 2
                        self.sy.wait_ge(self.PE.h, slab_free[ssl])
                        ws = self.dma("sy", slab[:, ssl * 16384:(ssl + 1) * 16384]
                                      .rearrange("p (a b) -> p a b", b=512),
                                      self.d_qkvT[ob].rearrange("(a p) b -> p a b",
                                                                p=128))
                        self.te.wait_ge(self.SY.h, ws)
                        slab3 = slab[:, ssl * 16384:(ssl + 1) * 16384].rearrange(
                            "p (a b) -> p a b", b=512)
                        for tt in range(4):
                            pbk = self.pb[tt]
                            self.te.wait_ge(self.VE.h, ln_done[tt])
                            self.bank_wait(self.te, tt)
                            self.t(lambda pbk=pbk, ob=ob: self.te.matmul(
                                pbk, self.ones[0:1, :],
                                self.qb[0:1, ob * 512:(ob + 1) * 512],
                                start=True, stop=False))
                            for hc in range(32):
                                self.t(lambda pbk=pbk, hc=hc, tt=tt, slab3=slab3:
                                       self.te.matmul(
                                           pbk,
                                           lnfm[:, hc * 512 + tt * 128:
                                                hc * 512 + tt * 128 + 128],
                                           slab3[:, hc, :],
                                           start=False, stop=(hc == 31)))
                            self._p1_evac(half, tb, tt, ob, self.PE.n)
                        slab_free[ssl] = self.PE.n
            del self.ropeT

    def _p1_evac(self, half, tb, tt, ob, pe_mm):
        pbk = self.pb[tt]
        if half == "kv":
            csrc, ssrc, row0 = self.i_cg, self.i_sg, tb * 512 + tt * 128
        else:
            csrc, ssrc, row0 = self.i_co, self.i_so, tb * 512 + tt * 128
        self.sc.wait_ge(self.VE.h, self.VE.n)
        self.dma("sc", self.cosT[:], csrc[row0:row0 + 128, :])
        wc = self.dma("sc", self.sinT[:], ssrc[row0:row0 + 128, :])
        self.ve.wait_ge(self.PE.h, pe_mm)
        self.ve.wait_ge(self.SC.h, wc)
        self.ve.wait_ge(self.SY.h, self.SY.n)
        if half == "kv":
            self.v(lambda: self.ve.tensor_copy(self.stg[:, 0:256], pbk[:, 0:256]))
            self._rope(self.stg[:, 0:256].rearrange("p (h d) -> p h d", d=128), 2)
            self.v(lambda: self.ve.tensor_copy(self.stgb[:, 0:256], self.stg[:, 0:256]))
            nv = self.v(lambda: self.ve.tensor_copy(self.stgb[:, 256:512],
                                                    pbk[:, 256:512]))
            self.bank_used(tt, "VE", nv)
            sec = tb * 4 + tt
            self.sy.wait_ge(self.VE.h, nv)
            self.dma("sy", self.d_V[:, sec * 256:(sec + 1) * 256],
                     self.stgb[:, 256:512])
            self.te.wait_ge(self.VE.h, nv)
            ins = None
            self.bank_wait(self.te, 6)
            for kh in range(2):
                ins = self.te.transpose(self.pbh[6][:, kh * 128:(kh + 1) * 128],
                                        self.stgb[:, kh * 128:(kh + 1) * 128],
                                        self.ident[:])
            pe_n = self.PE.inc(ins)
            self.ve.wait_ge(self.PE.h, pe_n)
            nk = self.v(lambda: self.ve.tensor_copy(self.ofm[:, 0:256],
                                                    self.pbh[6][:, 0:256]))
            self.bank_used(6, "VE", nk)
            self.sy.wait_ge(self.VE.h, nk)
            for kh in range(2):
                self.dma("sy", self.d_K[:, kh * S + tb * 512 + tt * 128:
                                        kh * S + tb * 512 + (tt + 1) * 128],
                         self.ofm[:, kh * 128:(kh + 1) * 128])
        else:
            sl = tb
            self.v(lambda: self.ve.tensor_copy(self.stg[:], pbk))
            self._rope(self.stg[:].rearrange("p (h d) -> p h d", d=128), 4)
            nb = self.v(lambda: self.ve.tensor_copy(self.stgb[:], self.stg[:]))
            self.bank_used(tt, "VE", nb)
            self.te.wait_ge(self.VE.h, nb)
            ins = None
            self.bank_wait(self.te, 6)
            for ot in range(4):
                ins = self.te.transpose(self.pbh[6][:, ot * 128:(ot + 1) * 128],
                                        self.stgb[:, ot * 128:(ot + 1) * 128],
                                        self.ident[:])
            pe_n = self.PE.inc(ins)
            self.ve.wait_ge(self.PE.h, pe_n)
            nq = self.v(lambda: self.ve.tensor_copy(self.ofm[:], self.pbh[6][:, 0:512]))
            self.bank_used(6, "VE", nq)
            self.sy.wait_ge(self.VE.h, nq)
            for ot in range(4):
                head = ob * 4 + ot
                self.dma("sy", self.d_Q[head, sl, :, tt * 128:(tt + 1) * 128],
                         self.ofm[:, ot * 128:(ot + 1) * 128])

    def p3(self):
        nc = self.nc
        with nc.sbuf_tensor("Ksb", [128, 2 * S], BF16) as Ksb, \
             nc.sbuf_tensor("Vsb", [128, 2 * S], BF16) as Vsb, \
             nc.sbuf_tensor("ctx", [128, 32 * 1024], BF16) as ctx, \
             nc.sbuf_tensor("msl", [128, 8 * 512], BF16) as msl, \
             nc.sbuf_tensor("Qh", [128, 2 * 512], BF16) as Qh, \
             nc.sbuf_tensor("Eb", [128, 2 * 1024], BF16) as Eb, \
             nc.sbuf_tensor("qsq", [128, 512], BF16) as qsq, \
             nc.sbuf_tensor("recd", [128, 512], F32) as recd, \
             nc.sbuf_tensor("dslab", [128, 32 * 512], BF16) as dslab, \
             nc.sbuf_tensor("x1t", [128, 2 * 512], F32) as x1t:
            wk = self.dma("sy", Ksb[:], self.d_K[:, :])
            wv = self.dma("sy", Vsb[:], self.d_V[:, :])
            self.sc.wait_ge(self.SY.h, wv)
            self._p3_kmax(Ksb, qsq)
            qh_hist = [0, 0]
            for sh in range(2):
                for sli in range(2):
                    sl = sh * 2 + sli
                    self.sy.wait_ge(self.VE.h, self.VE.n)
                    wm = self.dma("sy", msl[:].rearrange("p (a b) -> p a b", b=512),
                                  self.i_msk[sl].rearrange("a p b -> p a b"))
                    self.ve.wait_ge(self.SY.h, wm)
                    nu = NU[sl]
                    for head in range(NH):
                        kvh = head // 16
                        qs = head % 2
                        self.gp.wait_ge(self.PE.h, qh_hist[-2])
                        wq = self.dma("gp", Qh[:, qs * 512:(qs + 1) * 512],
                                      self.d_Q[head, sl, :, :])
                        self._p3_head(sl, head, nu, kvh, qs, wq, Ksb, Vsb, ctx,
                                      msl, Qh, Eb, qsq, recd, sli)
                        qh_hist.append(self.PE.n)
                self._p3_dense(sh, ctx, dslab, x1t)
            self.dump("ctx", ctx[:], [128, 32 * 1024], BF16)

    def _p3_kmax(self, Ksb, qsq):
        for kvh in range(2):
            for kb in range(8):
                self.sc.wait_ge(self.PE.h, self.PE.n)
                ns = self.s(lambda kvh=kvh, kb=kb: self.sc.activation(
                    out=qsq[:], in_=Ksb[:, kvh * S + kb * 512:
                                       kvh * S + (kb + 1) * 512], func=AF.Square))
                self.te.wait_ge(self.AC.h, ns)
                self.bank_wait(self.te, 6)
                pe_n = self.t(lambda: self.te.matmul(
                    self.pb[6], self.ones[:], qsq[:], start=True, stop=True))
                self.ve.wait_ge(self.PE.h, pe_n)
                dst = self.smA[:, 8:9] if kb == 0 else self.smA[:, 9:10]
                n = self.v(lambda dst=dst: self.ve.reduce_max(
                    dst, self.pb[6], axis=AX.X))
                self.bank_used(6, "VE", n)
                if kb > 0:
                    self.v(lambda: self.ve.tensor_tensor(
                        out=self.smA[:, 8:9], in0=self.smA[:, 8:9],
                        in1=self.smA[:, 9:10], op=ALU.max))
            self.sc.wait_ge(self.VE.h, self.VE.n)
            ns = self.s(lambda: self.sc.activation(
                out=self.smA[:, 10:11], in_=self.smA[:, 8:9], func=AF.Sqrt))
            self.ve.wait_ge(self.AC.h, ns)
            self.v(lambda kvh=kvh: self.ve.tensor_scalar(
                out=self.kmaxr[:, kvh:kvh + 1], in0=self.smA[:, 10:11],
                scalar1=RSCALE, scalar2=None, op0=ALU.mult))

    def _p3_head(self, sl, head, nu, kvh, qs, wq, Ksb, Vsb, ctx, msl, Qh, Eb,
                 qsq, recd, sli):
        self.sc.wait_ge(self.GP.h, wq)
        self.sc.wait_ge(self.PE.h, self.PE.n)
        ns = self.s(lambda qs=qs: self.sc.activation(
            out=qsq[:], in_=Qh[:, qs * 512:(qs + 1) * 512], func=AF.Square))
        self.te.wait_ge(self.AC.h, ns)
        self.bank_wait(self.te, 6)
        pe_n = self.t(lambda: self.te.matmul(
            self.pb[6], self.ones[:], qsq[:], start=True, stop=True))
        self.ve.wait_ge(self.PE.h, pe_n)
        n = self.v(lambda: self.ve.reduce_max(self.smA[:, 11:12], self.pb[6],
                                              axis=AX.X))
        self.bank_used(6, "VE", n)
        self.sc.wait_ge(self.VE.h, self.VE.n)
        ns = self.s(lambda: self.sc.activation(
            out=self.smA[:, 12:13], in_=self.smA[:, 11:12], func=AF.Sqrt))
        self.ve.wait_ge(self.AC.h, ns)
        nbias = self.v(lambda kvh=kvh: self.ve.scalar_tensor_tensor(
            out=self.smA[:, 13:14], in0=self.smA[:, 12:13], scalar=-1.0,
            in1=self.kmaxr[:, kvh:kvh + 1], op0=ALU.mult, op1=ALU.mult))
        self.sc.wait_ge(self.VE.h, nbias)
        exp_done = {}
        pvm_done = {}
        for kp in range(nu // 2):
            pgrp = kp % 2
            ebs = (kp % 2) * 1024
            self.bank_wait(self.te, pgrp * 2)
            for kj in range(2):
                kt = kp * 2 + kj
                self.t(lambda pgrp=pgrp, kj=kj, kvh=kvh, kt=kt, qs=qs:
                       self.te.matmul(
                           self.pp[pgrp][:, kj * 512:(kj + 1) * 512],
                           Ksb[:, kvh * S + kt * 128: kvh * S + (kt + 1) * 128],
                           Qh[:, qs * 512:(qs + 1) * 512],
                           start=True, stop=True))
            pe_st = self.PE.n
            self.sc.wait_ge(self.PE.h, pe_st)
            if kp >= 2:
                self.sc.wait_ge(self.PE.h, pvm_done[kp - 2])
            ne = self.s(lambda pgrp=pgrp, ebs=ebs: self.sc.activation(
                out=Eb[:, ebs:ebs + 1024], in_=self.pp[pgrp][:],
                func=AF.Exp, bias=self.smA[:, 13:14], scale=RSCALE))
            exp_done[kp] = ne
            self.bank_used(pgrp * 2, "AC", ne)
            last_ve = None
            for kj in range(2):
                kt = kp * 2 + kj
                if kt >= nu - 8:
                    j = kt - (nu - 8)
                    self.ve.wait_ge(self.AC.h, ne)
                    last_ve = self.v(lambda ebs=ebs, kj=kj, j=j:
                                     self.ve.tensor_tensor(
                        out=Eb[:, ebs + kj * 512: ebs + (kj + 1) * 512],
                        in0=Eb[:, ebs + kj * 512: ebs + (kj + 1) * 512],
                        in1=msl[:, j * 512:(j + 1) * 512], op=ALU.mult))
            if last_ve is not None:
                self.te.wait_ge(self.VE.h, last_ve)
            else:
                self.te.wait_ge(self.AC.h, ne)
            if kp == 0:
                self.bank_wait(self.te, 4)
                self.bank_wait(self.te, 5)
            for kj in range(2):
                kt = kp * 2 + kj
                self.t(lambda ebs=ebs, kj=kj, kt=kt, nu=nu: self.te.matmul(
                    self.pb[4], self.ones[:],
                    Eb[:, ebs + kj * 512: ebs + (kj + 1) * 512],
                    start=(kt == 0), stop=(kt == nu - 1)))
                self.t(lambda ebs=ebs, kj=kj, kt=kt, kvh=kvh, nu=nu:
                       self.te.matmul(
                    self.pb[5],
                    Vsb[:, kt * 256 + kvh * 128: kt * 256 + (kvh + 1) * 128],
                    Eb[:, ebs + kj * 512: ebs + (kj + 1) * 512],
                    start=(kt == 0), stop=(kt == nu - 1)))
            pvm_done[kp] = self.PE.n
        self.ve.wait_ge(self.PE.h, self.PE.n)
        self.v(lambda: self.ve.reciprocal(recd[:], self.pb[4]))
        nctx = self.v(lambda head=head, sli=sli: self.ve.tensor_tensor(
            out=ctx[:, head * 1024 + sli * 512: head * 1024 + (sli + 1) * 512],
            in0=self.pb[5], in1=recd[:], op=ALU.mult))
        self.bank_used(4, "VE", nctx)
        self.bank_used(5, "VE", nctx)

    def _p3_dense(self, sh, ctx, dslab, x1t):
        dn_evac = {}
        for ob in range(8):
            self.sy.wait_ge(self.PE.h, self.PE.n)
            ws = self.dma("sy", dslab[:].rearrange("p (a b) -> p a b", b=512),
                          self.d_dT[ob].rearrange("(a p) b -> p a b", p=128))
            self.te.wait_ge(self.SY.h, ws)
            dsl3 = dslab[:].rearrange("p (a b) -> p a b", b=512)
            for tt in range(8):
                row = sh * 8 + tt
                bk = tt % 4
                pbk = self.pb[bk]
                xs = tt % 2
                self.gp.wait_ge(self.VE.h, dn_evac.get((xs,), 0))
                self.gp.wait_ge(self.SY.h, self.SY.n)
                wx = self.dma("gp", x1t[:, xs * 512:(xs + 1) * 512],
                              self.i_xo[row * 128:(row + 1) * 128,
                                        ob * 512:(ob + 1) * 512])
                self.bank_wait(self.te, bk)
                for dc in range(32):
                    self.t(lambda pbk=pbk, dc=dc, tt=tt, dsl3=dsl3: self.te.matmul(
                        pbk, ctx[:, dc * 1024 + tt * 128: dc * 1024 + tt * 128 + 128],
                        dsl3[:, dc, :], start=(dc == 0), stop=(dc == 31)))
                pe_n = self.PE.n
                self.ve.wait_ge(self.PE.h, pe_n)
                self.ve.wait_ge(self.GP.h, wx)
                self.v(lambda pbk=pbk, xs=xs: self.ve.tensor_tensor(
                    out=x1t[:, xs * 512:(xs + 1) * 512], in0=pbk,
                    in1=x1t[:, xs * 512:(xs + 1) * 512], op=ALU.add))
                nev = self.v(lambda row=row, ob=ob, xs=xs: self.ve.bn_stats(
                    self.statsB[:, row * 48 + ob * 6: row * 48 + (ob + 1) * 6],
                    x1t[:, xs * 512:(xs + 1) * 512]))
                self.bank_used(bk, "VE", nev)
                dn_evac[(xs,)] = nev
                self.sy.wait_ge(self.VE.h, nev)
                self.dma("sy", self.d_x1[row * 128:(row + 1) * 128,
                                         ob * 512:(ob + 1) * 512],
                         x1t[:, xs * 512:(xs + 1) * 512])

    def p45(self):
        nc = self.nc
        for th in range(2):
            with nc.sbuf_tensor(f"lnfm2_{th}", [128, 32 * 1024], BF16) as lnfm2:
                self._p4_half(th, lnfm2)
                self.bar()
                self._p5_half(th, lnfm2)
                self.bar()

    def _p4_half(self, th, lnfm2):
        nc = self.nc
        with nc.sbuf_tensor(f"x1b_{th}", [128, 2 * H], F32) as x1b, \
             nc.sbuf_tensor(f"ln2tm_{th}", [128, 2 * H], BF16) as ln2tm, \
             nc.sbuf_tensor(f"xgt_{th}", [128, 32], F32) as xgt, \
             nc.sbuf_tensor(f"xgtb_{th}", [128, 32], BF16) as xgtb:
            x1_free = [0, 0]
            ln2_free = [0, 0]
            for tt in range(8):
                row = th * 8 + tt
                xs = tt % 2
                self.gp.wait_ge(self.VE.h, x1_free[xs])
                wx = self.dma("gp", x1b[:, xs * H:(xs + 1) * H],
                              self.d_x1[row * 128:(row + 1) * 128, :])
                self.v(lambda row=row: self.ve.bn_aggr(
                    self.smA[:, 0:2], self.statsB[:, row * 48:(row + 1) * 48]))
                self.v(lambda: self.ve.tensor_scalar(
                    out=self.smA[:, 2:3], in0=self.smA[:, 1:2], scalar1=EPS,
                    scalar2=None, op0=ALU.add))
                self.sc.wait_ge(self.VE.h, self.VE.n)
                self.s(lambda: self.sc.activation(
                    out=self.smA[:, 3:4], in_=self.smA[:, 2:3], func=AF.Sqrt))
                self.ve.wait_ge(self.AC.h, self.AC.n)
                self.v(lambda: self.ve.reciprocal(self.smA[:, 4:5], self.smA[:, 3:4]))
                self.ve.wait_ge(self.GP.h, wx)
                self.ve.wait_ge(self.PE.h, ln2_free[xs])
                nl = self.v(lambda xs=xs: self.ve.tensor_scalar(
                    out=ln2tm[:, xs * H:(xs + 1) * H], in0=x1b[:, xs * H:(xs + 1) * H],
                    scalar1=self.smA[:, 0:1], scalar2=self.smA[:, 4:5],
                    op0=ALU.subtract, op1=ALU.mult))
                x1_free[xs] = nl
                nxg = self.v(lambda xs=xs: self.ve.tensor_reduce(
                    xgt[:, 0:32],
                    ln2tm[:, xs * H:(xs + 1) * H].rearrange("p (a b) -> p a b", b=128),
                    AX.X, ALU.add))
                nxg = self.v(lambda: self.ve.tensor_copy(xgtb[:, 0:32], xgt[:, 0:32]))
                self.te.wait_ge(self.VE.h, nxg)
                self.bank_wait(self.te, 6)
                ins = self.te.transpose(self.pbh[6][0:32, 0:128],
                                        xgtb[:, 0:32], self.ident[:])
                pe_n = self.PE.inc(ins)
                self.ve.wait_ge(self.PE.h, pe_n)
                n = self.v(lambda row=row: self.ve.tensor_copy(
                    self.Xg[0:32, row * 128 % 2048:(row * 128 % 2048) + 128],
                    self.pbh[6][0:32, 0:128]))
                self.bank_used(6, "VE", n)
                lnfm23 = lnfm2[:].rearrange("p (hc t) -> p hc t", t=1024)
                for hg in range(8):
                    bk = 6 + (hg % 2)
                    self.bank_wait(self.te, bk)
                    for hj in range(4):
                        hc = hg * 4 + hj
                        ins = self.te.transpose(
                            self.pbh[bk][:, hj * 128:(hj + 1) * 128],
                            ln2tm[:, xs * H + hc * 128: xs * H + (hc + 1) * 128],
                            self.ident[:])
                    pe_n = self.PE.inc(ins)
                    self.ve.wait_ge(self.PE.h, pe_n)
                    n = self.v(lambda bk=bk, hg=hg, tt=tt, lnfm23=lnfm23:
                               self.ve.tensor_copy(
                        lnfm23[:, hg * 4:(hg + 1) * 4, tt * 128:(tt + 1) * 128],
                        self.pbh[bk][:, 0:512].rearrange("p (a b) -> p a b", b=128)))
                    self.bank_used(bk, "VE", n)
                ln2_free[xs] = self.PE.n

    def _p5_half(self, th, lnfm2):
        nc = self.nc
        with nc.sbuf_tensor(f"qrb_{th}", [128, 2 * 4096], I32) as qrb, \
             nc.sbuf_tensor(f"qmb_{th}", [128, 4096], I32) as qmb, \
             nc.sbuf_tensor(f"srb_{th}", [128, 2 * 4096], BF16) as srb, \
             nc.sbuf_tensor(f"wgb_{th}", [128, 2 * 4096], BF16) as wgb, \
             nc.sbuf_tensor(f"wub_{th}", [128, 2 * 4096], BF16) as wub, \
             nc.sbuf_tensor(f"nzb_{th}", [32, 2 * 256], BF16) as nzb, \
             nc.sbuf_tensor(f"silu_{th}", [128, 512], BF16) as silu, \
             nc.sbuf_tensor(f"intb_{th}", [128, 2 * 1024], BF16) as intb:
            gq3 = self.i_gq.rearrange("(c m) i -> m c i", m=16)
            uq3 = self.i_uq.rearrange("(c m) i -> m c i", m=16)
            st = {"qr": {}, "w": {}, "int": {}}
            self.gp.wait_ge(self.SC.h, self.qtile_done)
            self.gp.wait_ge(self.SC.h, self.qtile_done2)
            for ic in range(NCI):
                self._p5_ic(th, ic, lnfm2, qrb, qmb, srb, wgb, wub, nzb, silu,
                            intb, gq3, uq3, st)

    def _p5_ic(self, th, ic, lnfm2, qrb, qmb, srb, wgb, wub, nzb, silu, intb,
               gq3, uq3, st):
        wsl = ic % 2
        par = 0 if ic % 2 == 0 else 4
        mats = ((0, self.d_gqr2, self.d_sgt, wgb), (1, self.d_uqr2, self.d_sut, wub))
        half_off = wsl * 4096
        for mi, q3, dsc, wdst in mats:
            self.gp.wait_ge(self.VE.h, st["qr"].get(wsl, 0))
            self.dma("gp", qrb[:, half_off:half_off + 4096], q3[ic, :, :])
            self.gp.wait_ge(self.VE.h, st.setdefault("sr", {}).get(wsl, 0))
            w2 = self.dma("gp", srb[:, half_off:half_off + 4096],
                          dsc[None, ic, :].broadcast_to([128, 32 * 128]))
            st.setdefault("nz", {})[mi] = self.dma(
                "sy", nzb[:, wsl * 256 + mi * 128: wsl * 256 + (mi + 1) * 128],
                self.d_nzp[mi, :, ic * 128:(ic + 1) * 128])
            self.ve.wait_ge(self.GP.h, w2)
            self.v(lambda half_off=half_off: self.ve.tensor_scalar(
                out=qmb[:], in0=qrb[:, half_off:half_off + 4096],
                scalar1=self.shifts[:], scalar2=15,
                op0=ALU.logical_shift_right, op1=ALU.bitwise_and))
            st["qr"][wsl] = self.VE.n
            self.ve.wait_ge(self.PE.h, st["w"].get((wsl, mi), 0))
            nw = self.v(lambda wdst=wdst, half_off=half_off: self.ve.tensor_tensor(
                out=wdst[:, half_off:half_off + 4096], in0=qmb[:],
                in1=srb[:, half_off:half_off + 4096], op=ALU.mult))
            st["sr"][wsl] = nw
        self.te.wait_ge(self.VE.h, nw)
        self.te.wait_ge(self.SY.h, st["nz"][1])
        for mi, _, _, wsrc in mats:
            for tc in range(2):
                bk = mi * 2 + tc + par
                self.bank_wait(self.te, bk)
                pbk = self.pb[bk]
                self.t(lambda pbk=pbk, mi=mi, tc=tc, wsl=wsl: self.te.matmul(
                    pbk, nzb[:, wsl * 256 + mi * 128: wsl * 256 + (mi + 1) * 128],
                    self.Xg[0:32, th * 1024 + tc * 512: th * 1024 + (tc + 1) * 512],
                    start=True, stop=False))
                for c in range(32):
                    self.t(lambda pbk=pbk, c=c, tc=tc, wsrc=wsrc, half_off=half_off:
                           self.te.matmul(
                        pbk, wsrc[:, half_off + c * 128: half_off + (c + 1) * 128],
                        lnfm2[:, c * 1024 + tc * 512: c * 1024 + (tc + 1) * 512],
                        start=False, stop=(c == 31)))
        st["w"][(wsl, 0)] = self.PE.n
        st["w"][(wsl, 1)] = self.PE.n
        pe_mm = self.PE.n
        if th == 0 and ic == 0 and "ffn0" in self.dbg:
            self.dump("lnfm2h0", lnfm2[:], [128, 32 * 1024], BF16)
            self.dump("wgb0", wgb[:, 0:4096], [128, 4096], BF16)
            self.dump("nzb0", nzb[0:32, 0:128], [32, 128], BF16)
            self.dump("Xg0", self.Xg[0:32, :], [32, 2048], BF16)
            with self.nc.sbuf_tensor("dbgps", [128, 2048], F32) as dbgps:
                self.ve.wait_ge(self.PE.h, pe_mm)
                for b in range(4):
                    self.v(lambda b=b: self.ve.tensor_copy(
                        dbgps[:, b * 512:(b + 1) * 512], self.pb[b]))
                self.dump("gups0", dbgps[:], [128, 2048], F32)
                self.bar()
        for tc in range(2):
            bkg = 0 + tc + par
            bku = 2 + tc + par
            self.sc.wait_ge(self.PE.h, pe_mm)
            self.sc.wait_ge(self.VE.h, self.VE.n)
            nsil = self.s(lambda bkg=bkg: self.sc.activation(
                out=silu[:], in_=self.pb[bkg], func=AF.Silu))
            self.ve.wait_ge(self.AC.h, nsil)
            self.ve.wait_ge(self.SY.h, st["int"].get((wsl, tc), 0))
            nint = self.v(lambda bku=bku, tc=tc, wsl=wsl: self.ve.tensor_tensor(
                out=intb[:, wsl * 1024 + tc * 512: wsl * 1024 + (tc + 1) * 512],
                in0=silu[:], in1=self.pb[bku], op=ALU.mult))
            self.bank_used(bkg, "AC", nsil)
            self.bank_used(bku, "VE", nint)
            self.sy.wait_ge(self.VE.h, nint)
            st["int"][(wsl, tc)] = self.dma(
                "sy", self.d_int[ic, :, th * 1024 + tc * 512:
                                 th * 1024 + (tc + 1) * 512],
                intb[:, wsl * 1024 + tc * 512: wsl * 1024 + (tc + 1) * 512])

    def p6(self):
        nc = self.nc
        with nc.sbuf_tensor("intf", [128, NCI * 512], BF16) as intf, \
             nc.sbuf_tensor("wdt", [128, 4 * 512], BF16) as wdt, \
             nc.sbuf_tensor("f16t", [128, 512], F16) as f16t, \
             nc.sbuf_tensor("x1r", [128, 2 * 512], F32) as x1r, \
             nc.sbuf_tensor("yot", [128, 2 * 512], F32) as yot:
            for tc in range(4):
                wi = 0
                self.sy.wait_ge(self.PE.h, self.PE.n)
                for ic in range(NCI):
                    wi = self.dma("sy", intf[:, ic * 512:(ic + 1) * 512],
                                  self.d_int[ic, :, tc * 512:(tc + 1) * 512])
                self.te.wait_ge(self.SY.h, wi)
                wd_free = {}
                ev_done = {}
                for hb in range(8):
                    par = hb % 2
                    for ic in range(NCI):
                        dsl = ic % 4
                        self.gp.wait_ge(self.PE.h, wd_free.get(dsl, 0))
                        wwd = self.dma("gp", wdt[:, dsl * 512:(dsl + 1) * 512],
                                       self.d_wd[ic, :, hb * 512:(hb + 1) * 512])
                        self.te.wait_ge(self.GP.h, wwd)
                        for tt in range(4):
                            bk = par * 4 + tt
                            if ic == 0:
                                self.bank_wait(self.te, bk)
                            self.t(lambda bk=bk, ic=ic, tt=tt, dsl=dsl:
                                   self.te.matmul(
                                self.pb[bk],
                                intf[:, ic * 512 + tt * 128: ic * 512 + tt * 128 + 128],
                                wdt[:, dsl * 512:(dsl + 1) * 512],
                                start=(ic == 0), stop=(ic == NCI - 1)))
                        wd_free[dsl] = self.PE.n
                    pe_mm = self.PE.n
                    for tt in range(4):
                        bk = par * 4 + tt
                        row = tc * 4 + tt
                        xs = tt % 2
                        self.gp.wait_ge(self.VE.h, ev_done.get(xs, 0))
                        wx = self.dma("gp", x1r[:, xs * 512:(xs + 1) * 512],
                                      self.d_x1[row * 128:(row + 1) * 128,
                                                hb * 512:(hb + 1) * 512])
                        self.ve.wait_ge(self.PE.h, pe_mm)
                        self.ve.wait_ge(self.GP.h, wx)
                        self.ve.wait_ge(self.SY.h, self.SY.n)
                        self.v(lambda bk=bk: self.ve.tensor_copy(f16t[:], self.pb[bk]))
                        nev = self.v(lambda xs=xs: self.ve.tensor_tensor(
                            out=yot[:, xs * 512:(xs + 1) * 512], in0=f16t[:],
                            in1=x1r[:, xs * 512:(xs + 1) * 512], op=ALU.add))
                        self.bank_used(bk, "VE", nev)
                        ev_done[xs] = nev
                        self.sy.wait_ge(self.VE.h, nev)
                        self.dma("sy", self.o_y[row * 128:(row + 1) * 128,
                                                hb * 512:(hb + 1) * 512],
                                 yot[:, xs * 512:(xs + 1) * 512])

    def build(self, upto=None):
        self.p0_consts()
        self.p0_qrep()
        self.p0_scales()
        self.p0_zeros()
        self.bar()
        self.p0_wT()
        self.bar()
        self.p0_down()
        self.bar()
        if upto == "p0":
            return self.finish()
        self.p1()
        self.bar()
        if upto == "p1":
            return self.finish()
        self.p3()
        self.bar()
        if upto == "p3":
            return self.finish()
        self.p45()
        self.bar()
        if upto == "p5":
            return self.finish()
        self.p6()
        return self.finish()

    def finish(self):
        self.bar()
        self.sy.wait_ge(self.SY.h, self.SY.n)
        self.gp.wait_ge(self.GP.h, self.GP.n)
        self.sc.wait_ge(self.SC.h, self.SC.n)
        return self.nc


def shard_inputs(inputs):
    cos_g, sin_g, ident, shifts, masks = host_tables()
    x = np.ascontiguousarray(inputs["L0_attn_hidden_states"])
    qkv_w = np.ascontiguousarray(inputs["L0_attn_qkv_w"])
    qkv_b = np.ascontiguousarray(inputs["L0_attn_qkv_b"]).reshape(1, 4608)
    dense_w = np.ascontiguousarray(inputs["L0_attn_dense_w"])
    in_maps = []
    for core in range(8):
        b, half = core // 2, core % 2
        rows = np.concatenate([np.arange(g * CH, (g + 1) * CH)
                               for g in SLOTS[half]])
        im = dict(
            x_glob=np.ascontiguousarray(x[b]),
            x_own=np.ascontiguousarray(x[b][rows]),
            cos_g=cos_g, sin_g=sin_g,
            cos_o=np.ascontiguousarray(cos_g[rows]),
            sin_o=np.ascontiguousarray(sin_g[rows]),
            masks=masks[half], ident=ident, shifts=shifts,
            qkv_w=qkv_w, qkv_b=qkv_b, dense_w=dense_w,
            gate_q=np.ascontiguousarray(inputs["L0_ffn_gate_qweight"]),
            gate_s=np.ascontiguousarray(inputs["L0_ffn_gate_scales"]),
            gate_z=np.ascontiguousarray(inputs["L0_ffn_gate_zeros"]),
            up_q=np.ascontiguousarray(inputs["L0_ffn_up_qweight"]),
            up_s=np.ascontiguousarray(inputs["L0_ffn_up_scales"]),
            up_z=np.ascontiguousarray(inputs["L0_ffn_up_zeros"]),
            down_q=np.ascontiguousarray(inputs["L0_ffn_down_qweight"]),
            down_s=np.ascontiguousarray(inputs["L0_ffn_down_scales"]),
            down_z=np.ascontiguousarray(inputs["L0_ffn_down_zeros"]),
        )
        in_maps.append(im)
    return in_maps


def kernel(**inputs):
    in_maps = shard_inputs(inputs)
    kb = KB()
    nc = kb.build()
    res = run_bass_kernel_spmd(nc, in_maps, core_ids=list(range(8)))
    out = np.zeros((B, S, H), dtype=np.float32)
    for core in range(8):
        b, half = core // 2, core % 2
        y = res.results[core]["y_own"]
        for si, g in enumerate(SLOTS[half]):
            out[b, g * CH:(g + 1) * CH] = y[si * CH:(si + 1) * CH]
    return out


# version 07 (final)
# speedup vs baseline: 1.5020x; 1.4744x over previous
import numpy as np
import ml_dtypes
import concourse.bass as bass
import concourse.mybir as mybir
from concourse.bass_utils import run_bass_kernel_spmd

F32 = mybir.dt.float32
BF16 = mybir.dt.bfloat16
F16 = mybir.dt.float16
I32 = mybir.dt.int32
AF = mybir.ActivationFunctionType
ALU = mybir.AluOpType
AX = mybir.AxisListType
bf = ml_dtypes.bfloat16

B, S, H = 4, 4096, 4096
NH, KVH, D = 32, 2, 128
INTER = 11008
NCI = INTER // 128
CH = 512
SLOTS = [[0, 7, 3, 4], [1, 6, 2, 5]]
NU = [8, 32, 16, 24]
RSCALE = float(1.0 / np.sqrt(128.0))
EPS = 1e-5


def host_tables():
    inv = 1.0 / 10000.0 ** (np.arange(0, 64, 2, dtype=np.float64) / 64.0)
    pos = np.arange(S, dtype=np.float64)[:, None] * inv[None, :]
    cos_g = np.cos(pos).astype(np.float32)
    sin_g = np.sin(pos).astype(np.float32)
    ident = np.eye(128, dtype=bf)
    shifts = ((np.arange(128) % 8) * 4).astype(np.int32).reshape(128, 1)
    masks = []
    for half in (0, 1):
        m = np.zeros((4, 8, 128, 512), dtype=bf)
        for sl in range(4):
            g, nu = SLOTS[half][sl], NU[sl]
            nr = 4 * (g + 1)
            for j in range(8):
                kt = nu - 8 + j
                if kt >= nr:
                    continue
                kk = np.arange(128)[:, None]
                qq = np.arange(512)[None, :]
                m[sl, j] = ((g * CH + qq) >= (kt * 128 + kk)).astype(bf)
        masks.append(m)
    return cos_g, sin_g, ident, shifts, masks


class Sem:
    def __init__(self, nc, name):
        self.h = nc.alloc_semaphore(name=name)
        self.n = 0

    def inc(self, inst, k=1):
        inst.then_inc(self.h, k)
        self.n += k
        return self.n


class KB:
    def __init__(self, dbg=()):
        nc = bass.Bass("TRN2", target_bir_lowering=False)
        self.nc = nc
        self.dbg = set(dbg)
        self.dbg_t = {}
        self.sy, self.ve, self.sc, self.te, self.gp = \
            nc.sync, nc.vector, nc.scalar, nc.tensor, nc.gpsimd

        def inp(name, shape, dtype):
            return nc.dram_tensor(name, list(shape), dtype, kind="ExternalInput")
        self.i_xg = inp("x_glob", [S, H], F32)
        self.i_xo = inp("x_own", [S // 2, H], F32)
        self.i_cg = inp("cos_g", [S, 32], F32)
        self.i_sg = inp("sin_g", [S, 32], F32)
        self.i_co = inp("cos_o", [S // 2, 32], F32)
        self.i_so = inp("sin_o", [S // 2, 32], F32)
        self.i_msk = inp("masks", [4, 8, 128, 512], BF16)
        self.i_id = inp("ident", [128, 128], BF16)
        self.i_sh = inp("shifts", [128, 1], I32)
        self.i_qkvw = inp("qkv_w", [4608, H], F32)
        self.i_qkvb = inp("qkv_b", [1, 4608], F32)
        self.i_dw = inp("dense_w", [H, H], F32)
        self.i_gq = inp("gate_q", [512, INTER], I32)
        self.i_gs = inp("gate_s", [32, INTER], F16)
        self.i_gz = inp("gate_z", [32, INTER // 8], I32)
        self.i_uq = inp("up_q", [512, INTER], I32)
        self.i_us = inp("up_s", [32, INTER], F16)
        self.i_uz = inp("up_z", [32, INTER // 8], I32)
        self.i_dq = inp("down_q", [INTER // 8, H], I32)
        self.i_ds = inp("down_s", [NCI, H], F16)
        self.i_dz = inp("down_z", [NCI, H // 8], I32)
        self.o_y = nc.dram_tensor("y_own", [S // 2, H], F32, kind="ExternalOutput")

        D_ = lambda n, s, d: nc.dram_tensor(
            n, list(s), d,
            kind="ExternalOutput" if n in self.dbg else "Internal")
        self.d_qkvT = D_("d_qkvT", [9, H, 512], BF16)
        self.d_dT = D_("d_dT", [8, H, 512], BF16)
        self.d_K = D_("d_K", [128, 2 * S], BF16)
        self.d_V = D_("d_V", [128, 2 * S], BF16)
        self.d_Q = D_("d_Q", [NH, 4, 128, 512], BF16)
        self.d_x1 = D_("d_x1", [S // 2, H], F32)
        self.d_int = D_("d_int", [NCI, 128, S // 2], BF16)
        self.d_wd = D_("d_wd", [NCI, 128, H], BF16)
        self.d_zp = D_("d_zp", [NCI, H], BF16)
        self.d_nzp = D_("d_nzp", [2, 32, INTER], BF16)
        self.d_sg = D_("d_sg", [32, INTER], BF16)
        self.d_su = D_("d_su", [32, INTER], BF16)
        self.d_sd = D_("d_sd", [NCI, H], BF16)
        self.d_gqc = D_("d_gqc", [NCI, 128, 512], I32)
        self.d_uqc = D_("d_uqc", [NCI, 128, 512], I32)
        self.d_sgr = D_("d_sgr", [NCI, 128, 512], BF16)
        self.d_sur = D_("d_sur", [NCI, 128, 512], BF16)

        self.SY = Sem(nc, "sSY")
        self.GP = Sem(nc, "sGP")
        self.SC = Sem(nc, "sSC")
        self.VE = Sem(nc, "sVE")
        self.AC = Sem(nc, "sAC")
        self.PE = Sem(nc, "sPE")

        self.ident = nc.alloc_sbuf_tensor("sb_ident", [128, 128], BF16)
        self.ones = nc.alloc_sbuf_tensor("ones", [128, 128], BF16)
        self.shifts = nc.alloc_sbuf_tensor("sb_shifts", [128, 1], I32)
        self.statsB = nc.alloc_sbuf_tensor("statsB", [128, 16 * 48], F32)
        self.st48 = nc.alloc_sbuf_tensor("st48", [128, 48], F32)
        self.kmaxr = nc.alloc_sbuf_tensor("kmaxr", [128, 2], F32)
        self.smA = nc.alloc_sbuf_tensor("smA", [128, 16], F32)
        self.cosT = nc.alloc_sbuf_tensor("cosT", [128, 32], F32)
        self.sinT = nc.alloc_sbuf_tensor("sinT", [128, 32], F32)
        self.qb = nc.alloc_sbuf_tensor("qb", [1, 4608], BF16)
        self.Xg = nc.alloc_sbuf_tensor("Xg", [32, 2048], BF16)

        self.pp = [nc.alloc_psum_tensor(f"pp{i}", [128, 1024], F32) for i in range(2)]
        self.pb = [self.pp[0][:, 0:512], self.pp[0][:, 512:1024],
                   self.pp[1][:, 0:512], self.pp[1][:, 512:1024]]
        self._pbt = {}
        for i in range(4, 8):
            t = nc.alloc_psum_tensor(f"pb{i}", [128, 512], F32)
            self._pbt[i] = t
            self.pb.append(t[:, :])
        self.pbh = {i: self._pbt[i].bitcast(BF16) for i in (6, 7)}
        self.bank_free = {i: ("VE", 0) for i in range(8)}

    def v(self, emit):
        self.ve.wait_ge(self.VE.h, self.VE.n)
        return self.VE.inc(emit())

    def s(self, emit):
        self.sc.wait_ge(self.AC.h, self.AC.n)
        return self.AC.inc(emit())

    def t(self, emit):
        return self.PE.inc(emit())

    def sem(self, name):
        return {"SY": self.SY, "GP": self.GP, "SC": self.SC,
                "VE": self.VE, "AC": self.AC, "PE": self.PE}[name]

    def dma(self, q, out, in_):
        eng, sem = {"sy": (self.sy, self.SY), "gp": (self.gp, self.GP),
                    "sc": (self.sc, self.SC)}[q]
        return sem.inc(eng.dma_start(out=out, in_=in_), 16)

    def bank_wait(self, eng, bank):
        nm, cnt = self.bank_free[bank]
        if cnt > 0:
            eng.wait_ge(self.sem(nm).h, cnt)

    def bank_used(self, bank, sem_name, cnt):
        self.bank_free[bank] = (sem_name, cnt)

    def bar(self):
        self.nc.all_engine_barrier()
        for i in range(8):
            self.bank_free[i] = ("VE", 0)

    def dump(self, name, src_ap, shape, dtype):
        if name not in self.dbg:
            return
        if name not in self.dbg_t:
            self.dbg_t[name] = self.nc.dram_tensor(
                "dbg_" + name, list(shape), dtype, kind="ExternalOutput")
        idx = tuple(slice(None) for _ in shape)
        self.sy.wait_ge(self.VE.h, self.VE.n)
        self.sy.wait_ge(self.AC.h, self.AC.n)
        self.dma("sy", self.dbg_t[name][idx], src_ap)

    def dump_dram(self, name, dram_t, shape, dtype, via, n_parts=128):
        if name not in self.dbg:
            return
        t = self.nc.dram_tensor("dbg_" + name, list(shape), dtype,
                                kind="ExternalOutput")
        self.dbg_t[name] = t
        flat_src = dram_t.ap().flatten()
        flat_dst = t.ap().flatten()
        total = int(np.prod(shape))
        chunk = n_parts * 2048
        off = 0
        while off < total:
            n = min(chunk, total - off)
            rows = n // 2048 if n % 2048 == 0 else None
            if rows:
                w = self.dma("sy", via[0:rows, 0:2048],
                             flat_src[off:off + n].rearrange("(p f) -> p f", f=2048))
                self.gp.wait_ge(self.SY.h, w)
                self.dma("gp", flat_dst[off:off + n].rearrange("(p f) -> p f", f=2048),
                         via[0:rows, 0:2048])
            else:
                w = self.dma("sy", via[0:1, 0:n], flat_src[off:off + n][None, :])
                self.gp.wait_ge(self.SY.h, w)
                self.dma("gp", flat_dst[off:off + n][None, :], via[0:1, 0:n])
            off += n

    def p0_consts(self):
        self.dma("sy", self.ident[:], self.i_id[:, :])
        self.dma("sy", self.shifts[:], self.i_sh[:, :])
        self.dma("gp", self.qb[:], self.i_qkvb[:, :])
        self.v(lambda: self.ve.memset(self.ones[:], 1.0))

    def p0_qrep(self):
        for srci, dsti in ((self.i_gq, self.d_gqc), (self.i_uq, self.d_uqc)):
            for t in range(4):
                self.dma("sc",
                         dsti[:, :, t * 128:(t + 1) * 128]
                         .rearrange("ic p i -> p ic i"),
                         srci[t * 128:(t + 1) * 128, :]
                         .rearrange("p (ic i) -> p ic i", i=128))
        self.qtile_done = self.SC.n

    def p0_scales(self):
        nc = self.nc
        with nc.sbuf_tensor("sbf", [128, INTER], BF16) as sbf:
            for src, dst, G, N in ((self.i_gs, self.d_sg, 32, INTER),
                                   (self.i_us, self.d_su, 32, INTER),
                                   (self.i_ds, self.d_sd, NCI, H)):
                self.gp.wait_ge(self.SY.h, self.SY.n)
                w = self.dma("gp", sbf[0:G, 0:N], src[:, :])
                self.sy.wait_ge(self.GP.h, w)
                self.dma("sy", dst[:, :], sbf[0:G, 0:N])
            self.gp.wait_ge(self.SY.h, self.SY.n)
        self.sc.wait_ge(self.SY.h, self.SY.n)
        for srci, dsti in ((self.d_sg, self.d_sgr), (self.d_su, self.d_sur)):
            for t in range(4):
                for c8 in range(8):
                    r = 8 * t + c8
                    self.dma("sc",
                             dsti[:, c8 * 16:(c8 + 1) * 16, t * 128:(t + 1) * 128]
                             .rearrange("ic p i -> p ic i"),
                             srci[r:r + 1, :].rearrange("o (ic i) -> o ic i", i=128)
                             .broadcast_to([16, NCI, 128]))
        self.qtile_done2 = self.SC.n

    def p0_zeros(self):
        nc = self.nc
        with nc.sbuf_tensor("zpk", [128, INTER // 8], I32) as zpk, \
             nc.sbuf_tensor("zun", [128, INTER], I32) as zun, \
             nc.sbuf_tensor("zsc", [128, INTER], BF16) as zsc, \
             nc.sbuf_tensor("zot", [128, INTER], BF16) as zot:
            jobs = [(self.i_gz, self.i_gs, 32, INTER, -1.0, self.d_nzp[0]),
                    (self.i_uz, self.i_us, 32, INTER, -1.0, self.d_nzp[1]),
                    (self.i_dz, self.i_ds, NCI, H, 1.0, self.d_zp[:, :])]
            for z_in, s_in, G, N, sgn, dst in jobs:
                w1 = self.dma("gp", zpk[0:G, 0:N // 8], z_in[:, :])
                w2 = self.dma("gp", zsc[0:G, 0:N], s_in[:, :])
                self.ve.wait_ge(self.GP.h, max(w1, w2))
                for j in range(8):
                    self.v(lambda j=j, G=G, N=N: self.ve.tensor_scalar(
                        out=zun[0:G, 0:N][:, j::8], in0=zpk[0:G, 0:N // 8],
                        scalar1=j * 4, scalar2=15,
                        op0=ALU.logical_shift_right, op1=ALU.bitwise_and))
                self.v(lambda G=G, N=N: self.ve.tensor_scalar(
                    out=zun[0:G, 0:N], in0=zun[0:G, 0:N], scalar1=1, scalar2=None,
                    op0=ALU.add))
                n = self.v(lambda G=G, N=N, sgn=sgn: self.ve.scalar_tensor_tensor(
                    out=zot[0:G, 0:N], in0=zun[0:G, 0:N], scalar=sgn,
                    in1=zsc[0:G, 0:N], op0=ALU.mult, op1=ALU.mult))
                self.sy.wait_ge(self.VE.h, n)
                self.dma("sy", dst, zot[0:G, 0:N])
                self.gp.wait_ge(self.SY.h, self.SY.n)

    def p0_wd(self):
        nc = self.nc
        with nc.sbuf_tensor("wblk", [128, 4 * H], BF16) as wblk, \
             nc.sbuf_tensor("wst", [128, 2 * 512], BF16) as wst, \
             nc.sbuf_tensor("qr", [128, 2 * H], I32) as qr, \
             nc.sbuf_tensor("qm", [128, H], I32) as qm, \
             nc.sbuf_tensor("srp", [128, 2 * H], BF16) as srp, \
             nc.sbuf_tensor("zrp", [128, 2 * H], BF16) as zrp, \
             nc.sbuf_tensor("tmpf", [128, H], F32) as tmpf, \
             nc.sbuf_tensor("wdo", [128, 2 * H], BF16) as wdo:
            stw = {"wblk_free": 0}
            std = {"ve_done": {}, "dq": self.i_dq.rearrange("(c m) i -> m c i", m=16)}
            units = [(s, d, ob) for s, d, nob in
                     ((self.i_qkvw, self.d_qkvT, 9), (self.i_dw, self.d_dT, 8))
                     for ob in range(nob)]
            dc = 0
            for ui, (srcw, dstw, ob) in enumerate(units):
                self._p0_wt_unit(srcw, dstw, ob, wblk, wst, stw)
                dtarget = (ui + 1) * NCI // len(units)
                while dc < dtarget:
                    self._p0_down_chunk(dc, qr, qm, srp, zrp, tmpf, wdo, std)
                    dc += 1
            while dc < NCI:
                self._p0_down_chunk(dc, qr, qm, srp, zrp, tmpf, wdo, std)
                dc += 1

    def _p0_wt_unit(self, src, dst, ob, wblk, wst, stw):
        self.gp.wait_ge(self.PE.h, stw["wblk_free"])
        for ot in range(4):
            self.dma("gp", wblk[:, ot * H:(ot + 1) * H],
                     src[ob * 512 + ot * 128: ob * 512 + (ot + 1) * 128, :])
        w = self.GP.n
        self.te.wait_ge(self.GP.h, w)
        for hc in range(32):
            bk = 6 + (hc % 2)
            self.bank_wait(self.te, bk)
            ins = None
            for ot in range(4):
                ins = self.te.transpose(
                    self.pbh[bk][:, ot * 128:(ot + 1) * 128],
                    wblk[:, ot * H + hc * 128: ot * H + (hc + 1) * 128],
                    self.ident[:])
            pe_n = self.PE.inc(ins)
            self.ve.wait_ge(self.PE.h, pe_n)
            self.ve.wait_ge(self.SY.h, stw.setdefault("st", {}).get(hc % 2, 0))
            n = self.v(lambda bk=bk, hc=hc, wst=wst: self.ve.tensor_copy(
                wst[:, (hc % 2) * 512:(hc % 2) * 512 + 512],
                self.pbh[bk][:, 0:512]))
            self.bank_used(bk, "VE", n)
            self.sy.wait_ge(self.VE.h, n)
            stw["st"][hc % 2] = self.dma(
                "sy", dst[ob, hc * 128:(hc + 1) * 128, :],
                wst[:, (hc % 2) * 512:(hc % 2) * 512 + 512])
        stw["wblk_free"] = self.PE.n

    def _p0_down_chunk(self, c, qr, qm, srp, zrp, tmpf, wdo, std):
        sl = c % 2
        if c >= 2:
            self.gp.wait_ge(self.VE.h, std["ve_done"][c - 2])
        self.dma("gp", qr[:, sl * H:(sl + 1) * H],
                 std["dq"][:, None, c:c + 1, :].broadcast_to([16, 8, 1, H]))
        self.dma("gp", srp[:, sl * H:(sl + 1) * H],
                 self.d_sd[None, c, :].broadcast_to([128, H]))
        w3 = self.dma("gp", zrp[:, sl * H:(sl + 1) * H],
                      self.d_zp[None, c, :].broadcast_to([128, H]))
        self.ve.wait_ge(self.GP.h, w3)
        self.ve.wait_ge(self.SY.h, std.setdefault("st", {}).get(sl, 0))
        self.v(lambda sl=sl, qr=qr, qm=qm: self.ve.tensor_scalar(
            out=qm[:], in0=qr[:, sl * H:(sl + 1) * H], scalar1=self.shifts[:],
            scalar2=15, op0=ALU.logical_shift_right, op1=ALU.bitwise_and))
        self.v(lambda sl=sl, qm=qm, srp=srp, tmpf=tmpf: self.ve.tensor_tensor(
            out=tmpf[:], in0=qm[:], in1=srp[:, sl * H:(sl + 1) * H], op=ALU.mult))
        n = self.v(lambda sl=sl, tmpf=tmpf, zrp=zrp, wdo=wdo: self.ve.tensor_tensor(
            out=wdo[:, sl * H:(sl + 1) * H], in0=tmpf[:],
            in1=zrp[:, sl * H:(sl + 1) * H], op=ALU.subtract))
        std["ve_done"][c] = n
        self.sy.wait_ge(self.VE.h, n)
        std["st"][sl] = self.dma("sy", self.d_wd[c, :, :],
                                 wdo[:, sl * H:(sl + 1) * H])

    def _ln_tile(self, xt_ap, out_bf_ap):
        for kb in range(8):
            self.v(lambda kb=kb: self.ve.bn_stats(
                self.st48[:, kb * 6:(kb + 1) * 6], xt_ap[:, kb * 512:(kb + 1) * 512]))
        self.v(lambda: self.ve.bn_aggr(self.smA[:, 0:2], self.st48[:]))
        self.v(lambda: self.ve.tensor_scalar(
            out=self.smA[:, 2:3], in0=self.smA[:, 1:2], scalar1=EPS, scalar2=None,
            op0=ALU.add))
        self.sc.wait_ge(self.VE.h, self.VE.n)
        self.s(lambda: self.sc.activation(
            out=self.smA[:, 3:4], in_=self.smA[:, 2:3], func=AF.Sqrt))
        self.ve.wait_ge(self.AC.h, self.AC.n)
        self.v(lambda: self.ve.reciprocal(self.smA[:, 4:5], self.smA[:, 3:4]))
        return self.v(lambda: self.ve.tensor_scalar(
            out=out_bf_ap, in0=xt_ap, scalar1=self.smA[:, 0:1],
            scalar2=self.smA[:, 4:5], op0=ALU.subtract, op1=ALU.mult))

    def _rope(self, st3, nheads):
        e = st3[:, :, 0:64:2]
        o = st3[:, :, 1:64:2]
        c = self.cosT[:, None, :].broadcast_to([128, nheads, 32])
        s_ = self.sinT[:, None, :].broadcast_to([128, nheads, 32])

        def sv(i):
            return self.ropeT[:, i * 32 * nheads:(i + 1) * 32 * nheads].rearrange(
                "p (h d) -> p h d", d=32)
        self.v(lambda: self.ve.tensor_tensor(out=sv(0), in0=e, in1=c, op=ALU.mult))
        self.v(lambda: self.ve.tensor_tensor(out=sv(1), in0=e, in1=s_, op=ALU.mult))
        self.v(lambda: self.ve.tensor_tensor(out=sv(2), in0=o, in1=s_, op=ALU.mult))
        self.v(lambda: self.ve.tensor_tensor(out=sv(3), in0=o, in1=c, op=ALU.mult))
        self.v(lambda: self.ve.tensor_tensor(out=e, in0=sv(0), in1=sv(2),
                                             op=ALU.subtract))
        return self.v(lambda: self.ve.tensor_tensor(out=o, in0=sv(3), in1=sv(1),
                                                    op=ALU.add))

    def p1(self):
        nc = self.nc
        with nc.sbuf_tensor("xt", [128, 2 * H], F32) as xt, \
             nc.sbuf_tensor("lntm", [128, 2 * H], BF16) as lntm, \
             nc.sbuf_tensor("lnfm", [128, 32 * 512], BF16) as lnfm, \
             nc.sbuf_tensor("slab", [128, 2 * 32 * 512], BF16) as slab, \
             nc.sbuf_tensor("stg", [128, 512], F32) as stg, \
             nc.sbuf_tensor("stgb", [128, 512], BF16) as stgb, \
             nc.sbuf_tensor("ropeTb", [128, 512], F32) as ropeTb, \
             nc.sbuf_tensor("ofm", [128, 512], BF16) as ofm:
            self.ropeT = ropeTb
            self.stg, self.stgb, self.ofm = stg, stgb, ofm
            slab_free = [0, 0]
            xt_free = [0, 0]
            lntm_free = [0, 0]
            for half in ("kv", "q"):
                nblocks = 8 if half == "kv" else 4
                for tb in range(nblocks):
                    xsrc = self.i_xg if half == "kv" else self.i_xo
                    ln_done = {}
                    for tt in range(4):
                        xs = tt % 2
                        self.gp.wait_ge(self.VE.h, xt_free[xs])
                        wx = self.dma("gp", xt[:, xs * H:(xs + 1) * H],
                                      xsrc[tb * 512 + tt * 128:
                                           tb * 512 + (tt + 1) * 128, :])
                        self.ve.wait_ge(self.GP.h, wx)
                        self.ve.wait_ge(self.PE.h, lntm_free[xs])
                        nln = self._ln_tile(xt[:, xs * H:(xs + 1) * H],
                                            lntm[:, xs * H:(xs + 1) * H])
                        xt_free[xs] = nln
                        self.te.wait_ge(self.VE.h, nln)
                        lnfm3 = lnfm[:].rearrange("p (hc t) -> p hc t", t=512)
                        for hg in range(8):
                            bk = 6 + (hg % 2)
                            self.bank_wait(self.te, bk)
                            for hj in range(4):
                                hc = hg * 4 + hj
                                ins = self.te.transpose(
                                    self.pbh[bk][:, hj * 128:(hj + 1) * 128],
                                    lntm[:, xs * H + hc * 128: xs * H + (hc + 1) * 128],
                                    self.ident[:])
                            pe_n = self.PE.inc(ins)
                            self.ve.wait_ge(self.PE.h, pe_n)
                            n = self.v(lambda bk=bk, hg=hg, tt=tt, lnfm3=lnfm3:
                                       self.ve.tensor_copy(
                                lnfm3[:, hg * 4:(hg + 1) * 4, tt * 128:(tt + 1) * 128],
                                self.pbh[bk][:, 0:512].rearrange(
                                    "p (a b) -> p a b", b=128)))
                            self.bank_used(bk, "VE", n)
                        lntm_free[xs] = self.PE.n
                        ln_done[tt] = self.VE.n
                    obs = [8] if half == "kv" else list(range(8))
                    for obi, ob in enumerate(obs):
                        ssl = obi % 2
                        self.sy.wait_ge(self.PE.h, slab_free[ssl])
                        ws = self.dma("sy", slab[:, ssl * 16384:(ssl + 1) * 16384]
                                      .rearrange("p (a b) -> p a b", b=512),
                                      self.d_qkvT[ob].rearrange("(a p) b -> p a b",
                                                                p=128))
                        self.te.wait_ge(self.SY.h, ws)
                        slab3 = slab[:, ssl * 16384:(ssl + 1) * 16384].rearrange(
                            "p (a b) -> p a b", b=512)
                        mm_done = {}
                        for tt in range(4):
                            pbk = self.pb[tt]
                            self.te.wait_ge(self.VE.h, ln_done[tt])
                            self.bank_wait(self.te, tt)
                            self.t(lambda pbk=pbk, ob=ob: self.te.matmul(
                                pbk, self.ones[0:1, :],
                                self.qb[0:1, ob * 512:(ob + 1) * 512],
                                start=True, stop=False))
                            for hc in range(32):
                                self.t(lambda pbk=pbk, hc=hc, tt=tt, slab3=slab3:
                                       self.te.matmul(
                                           pbk,
                                           lnfm[:, hc * 512 + tt * 128:
                                                hc * 512 + tt * 128 + 128],
                                           slab3[:, hc, :],
                                           start=False, stop=(hc == 31)))
                            mm_done[tt] = self.PE.n
                            if tt > 0:
                                self._p1_evac(half, tb, tt - 1, ob, mm_done[tt - 1])
                        self._p1_evac(half, tb, 3, ob, mm_done[3])
                        slab_free[ssl] = self.PE.n
            del self.ropeT

    def _p1_evac(self, half, tb, tt, ob, pe_mm):
        pbk = self.pb[tt]
        if half == "kv":
            csrc, ssrc, row0 = self.i_cg, self.i_sg, tb * 512 + tt * 128
        else:
            csrc, ssrc, row0 = self.i_co, self.i_so, tb * 512 + tt * 128
        self.sc.wait_ge(self.VE.h, self.VE.n)
        self.dma("sc", self.cosT[:], csrc[row0:row0 + 128, :])
        wc = self.dma("sc", self.sinT[:], ssrc[row0:row0 + 128, :])
        self.ve.wait_ge(self.PE.h, pe_mm)
        self.ve.wait_ge(self.SC.h, wc)
        self.ve.wait_ge(self.SY.h, self.SY.n)
        if half == "kv":
            self.v(lambda: self.ve.tensor_copy(self.stg[:, 0:256], pbk[:, 0:256]))
            self._rope(self.stg[:, 0:256].rearrange("p (h d) -> p h d", d=128), 2)
            self.v(lambda: self.ve.tensor_copy(self.stgb[:, 0:256], self.stg[:, 0:256]))
            nv = self.v(lambda: self.ve.tensor_copy(self.stgb[:, 256:512],
                                                    pbk[:, 256:512]))
            self.bank_used(tt, "VE", nv)
            sec = tb * 4 + tt
            self.sy.wait_ge(self.VE.h, nv)
            self.dma("sy", self.d_V[:, sec * 256:(sec + 1) * 256],
                     self.stgb[:, 256:512])
            self.te.wait_ge(self.VE.h, nv)
            ins = None
            self.bank_wait(self.te, 6)
            for kh in range(2):
                ins = self.te.transpose(self.pbh[6][:, kh * 128:(kh + 1) * 128],
                                        self.stgb[:, kh * 128:(kh + 1) * 128],
                                        self.ident[:])
            pe_n = self.PE.inc(ins)
            self.ve.wait_ge(self.PE.h, pe_n)
            nk = self.v(lambda: self.ve.tensor_copy(self.ofm[:, 0:256],
                                                    self.pbh[6][:, 0:256]))
            self.bank_used(6, "VE", nk)
            self.sy.wait_ge(self.VE.h, nk)
            for kh in range(2):
                self.dma("sy", self.d_K[:, kh * S + tb * 512 + tt * 128:
                                        kh * S + tb * 512 + (tt + 1) * 128],
                         self.ofm[:, kh * 128:(kh + 1) * 128])
        else:
            sl = tb
            self.v(lambda: self.ve.tensor_copy(self.stg[:], pbk))
            self._rope(self.stg[:].rearrange("p (h d) -> p h d", d=128), 4)
            nb = self.v(lambda: self.ve.tensor_copy(self.stgb[:], self.stg[:]))
            self.bank_used(tt, "VE", nb)
            self.te.wait_ge(self.VE.h, nb)
            ins = None
            self.bank_wait(self.te, 6)
            for ot in range(4):
                ins = self.te.transpose(self.pbh[6][:, ot * 128:(ot + 1) * 128],
                                        self.stgb[:, ot * 128:(ot + 1) * 128],
                                        self.ident[:])
            pe_n = self.PE.inc(ins)
            self.ve.wait_ge(self.PE.h, pe_n)
            nq = self.v(lambda: self.ve.tensor_copy(self.ofm[:], self.pbh[6][:, 0:512]))
            self.bank_used(6, "VE", nq)
            self.sy.wait_ge(self.VE.h, nq)
            for ot in range(4):
                head = ob * 4 + ot
                self.dma("sy", self.d_Q[head, sl, :, tt * 128:(tt + 1) * 128],
                         self.ofm[:, ot * 128:(ot + 1) * 128])

    def p3(self):
        nc = self.nc
        with nc.sbuf_tensor("Ksb", [128, 2 * S], BF16) as Ksb, \
             nc.sbuf_tensor("Vsb", [128, 2 * S], BF16) as Vsb, \
             nc.sbuf_tensor("ctx", [128, 32 * 1024], BF16) as ctx, \
             nc.sbuf_tensor("msl", [128, 8 * 512], BF16) as msl, \
             nc.sbuf_tensor("Qh", [128, 2 * 512], BF16) as Qh, \
             nc.sbuf_tensor("Eb", [128, 2 * 1024], BF16) as Eb, \
             nc.sbuf_tensor("qsq", [128, 512], BF16) as qsq, \
             nc.sbuf_tensor("recd", [128, 512], F32) as recd, \
             nc.sbuf_tensor("dslab", [128, 32 * 512], BF16) as dslab, \
             nc.sbuf_tensor("x1t", [128, 2 * 512], F32) as x1t:
            wk = self.dma("sy", Ksb[:], self.d_K[:, :])
            wv = self.dma("sy", Vsb[:], self.d_V[:, :])
            self.sc.wait_ge(self.SY.h, wv)
            self._p3_kmax(Ksb, qsq)
            qh_hist = [0, 0]
            for sh in range(2):
                for sli in range(2):
                    sl = sh * 2 + sli
                    self.sy.wait_ge(self.VE.h, self.VE.n)
                    wm = self.dma("sy", msl[:].rearrange("p (a b) -> p a b", b=512),
                                  self.i_msk[sl].rearrange("a p b -> p a b"))
                    self.ve.wait_ge(self.SY.h, wm)
                    nu = NU[sl]
                    for head in range(NH):
                        kvh = head // 16
                        qs = head % 2
                        self.gp.wait_ge(self.PE.h, qh_hist[-2])
                        wq = self.dma("gp", Qh[:, qs * 512:(qs + 1) * 512],
                                      self.d_Q[head, sl, :, :])
                        self._p3_head(sl, head, nu, kvh, qs, wq, Ksb, Vsb, ctx,
                                      msl, Qh, Eb, qsq, recd, sli)
                        qh_hist.append(self.PE.n)
                self._p3_dense(sh, ctx, dslab, x1t)
            self.dump("ctx", ctx[:], [128, 32 * 1024], BF16)

    def _p3_kmax(self, Ksb, qsq):
        for kvh in range(2):
            for kb in range(8):
                self.sc.wait_ge(self.PE.h, self.PE.n)
                ns = self.s(lambda kvh=kvh, kb=kb: self.sc.activation(
                    out=qsq[:], in_=Ksb[:, kvh * S + kb * 512:
                                       kvh * S + (kb + 1) * 512], func=AF.Square))
                self.te.wait_ge(self.AC.h, ns)
                self.bank_wait(self.te, 6)
                pe_n = self.t(lambda: self.te.matmul(
                    self.pb[6], self.ones[:], qsq[:], start=True, stop=True))
                self.ve.wait_ge(self.PE.h, pe_n)
                dst = self.smA[:, 8:9] if kb == 0 else self.smA[:, 9:10]
                n = self.v(lambda dst=dst: self.ve.reduce_max(
                    dst, self.pb[6], axis=AX.X))
                self.bank_used(6, "VE", n)
                if kb > 0:
                    self.v(lambda: self.ve.tensor_tensor(
                        out=self.smA[:, 8:9], in0=self.smA[:, 8:9],
                        in1=self.smA[:, 9:10], op=ALU.max))
            self.sc.wait_ge(self.VE.h, self.VE.n)
            ns = self.s(lambda: self.sc.activation(
                out=self.smA[:, 10:11], in_=self.smA[:, 8:9], func=AF.Sqrt))
            self.ve.wait_ge(self.AC.h, ns)
            self.v(lambda kvh=kvh: self.ve.tensor_scalar(
                out=self.kmaxr[:, kvh:kvh + 1], in0=self.smA[:, 10:11],
                scalar1=RSCALE, scalar2=None, op0=ALU.mult))

    def _p3_head(self, sl, head, nu, kvh, qs, wq, Ksb, Vsb, ctx, msl, Qh, Eb,
                 qsq, recd, sli):
        self.sc.wait_ge(self.GP.h, wq)
        self.sc.wait_ge(self.PE.h, self.PE.n)
        ns = self.s(lambda qs=qs: self.sc.activation(
            out=qsq[:], in_=Qh[:, qs * 512:(qs + 1) * 512], func=AF.Square))
        self.te.wait_ge(self.AC.h, ns)
        self.bank_wait(self.te, 6)
        pe_n = self.t(lambda: self.te.matmul(
            self.pb[6], self.ones[:], qsq[:], start=True, stop=True))
        self.ve.wait_ge(self.PE.h, pe_n)
        n = self.v(lambda: self.ve.reduce_max(self.smA[:, 11:12], self.pb[6],
                                              axis=AX.X))
        self.bank_used(6, "VE", n)
        self.sc.wait_ge(self.VE.h, self.VE.n)
        ns = self.s(lambda: self.sc.activation(
            out=self.smA[:, 12:13], in_=self.smA[:, 11:12], func=AF.Sqrt))
        self.ve.wait_ge(self.AC.h, ns)
        nbias = self.v(lambda kvh=kvh: self.ve.scalar_tensor_tensor(
            out=self.smA[:, 13:14], in0=self.smA[:, 12:13], scalar=-1.0,
            in1=self.kmaxr[:, kvh:kvh + 1], op0=ALU.mult, op1=ALU.mult))
        self.sc.wait_ge(self.VE.h, nbias)
        exp_done = {}
        mask_done = {}
        pvm_done = {}
        npair = nu // 2

        def den_pv(kp):
            ebs = (kp % 2) * 1024
            if mask_done.get(kp) is not None:
                self.te.wait_ge(self.VE.h, mask_done[kp])
            else:
                self.te.wait_ge(self.AC.h, exp_done[kp])
            if kp == 0:
                self.bank_wait(self.te, 4)
                self.bank_wait(self.te, 5)
            for kj in range(2):
                kt = kp * 2 + kj
                self.t(lambda ebs=ebs, kj=kj, kt=kt, nu=nu: self.te.matmul(
                    self.pb[4], self.ones[:],
                    Eb[:, ebs + kj * 512: ebs + (kj + 1) * 512],
                    start=(kt == 0), stop=(kt == nu - 1)))
                self.t(lambda ebs=ebs, kj=kj, kt=kt, kvh=kvh, nu=nu:
                       self.te.matmul(
                    self.pb[5],
                    Vsb[:, kt * 256 + kvh * 128: kt * 256 + (kvh + 1) * 128],
                    Eb[:, ebs + kj * 512: ebs + (kj + 1) * 512],
                    start=(kt == 0), stop=(kt == nu - 1)))
            pvm_done[kp] = self.PE.n

        for kp in range(npair):
            pgrp = kp % 2
            ebs = (kp % 2) * 1024
            self.bank_wait(self.te, pgrp * 2)
            for kj in range(2):
                kt = kp * 2 + kj
                self.t(lambda pgrp=pgrp, kj=kj, kvh=kvh, kt=kt, qs=qs:
                       self.te.matmul(
                           self.pp[pgrp][:, kj * 512:(kj + 1) * 512],
                           Ksb[:, kvh * S + kt * 128: kvh * S + (kt + 1) * 128],
                           Qh[:, qs * 512:(qs + 1) * 512],
                           start=True, stop=True))
            pe_st = self.PE.n
            self.sc.wait_ge(self.PE.h, pe_st)
            if kp >= 2:
                self.sc.wait_ge(self.PE.h, pvm_done[kp - 2])
            ne = self.s(lambda pgrp=pgrp, ebs=ebs: self.sc.activation(
                out=Eb[:, ebs:ebs + 1024], in_=self.pp[pgrp][:],
                func=AF.Exp, bias=self.smA[:, 13:14], scale=RSCALE))
            exp_done[kp] = ne
            self.bank_used(pgrp * 2, "AC", ne)
            last_ve = None
            for kj in range(2):
                kt = kp * 2 + kj
                if kt >= nu - 8:
                    j = kt - (nu - 8)
                    self.ve.wait_ge(self.AC.h, ne)
                    last_ve = self.v(lambda ebs=ebs, kj=kj, j=j:
                                     self.ve.tensor_tensor(
                        out=Eb[:, ebs + kj * 512: ebs + (kj + 1) * 512],
                        in0=Eb[:, ebs + kj * 512: ebs + (kj + 1) * 512],
                        in1=msl[:, j * 512:(j + 1) * 512], op=ALU.mult))
            mask_done[kp] = last_ve
            if kp > 0:
                den_pv(kp - 1)
        den_pv(npair - 1)
        self.ve.wait_ge(self.PE.h, self.PE.n)
        self.v(lambda: self.ve.reciprocal(recd[:], self.pb[4]))
        nctx = self.v(lambda head=head, sli=sli: self.ve.tensor_tensor(
            out=ctx[:, head * 1024 + sli * 512: head * 1024 + (sli + 1) * 512],
            in0=self.pb[5], in1=recd[:], op=ALU.mult))
        self.bank_used(4, "VE", nctx)
        self.bank_used(5, "VE", nctx)

    def _p3_dense(self, sh, ctx, dslab, x1t):
        dn_evac = {}
        for ob in range(8):
            self.sy.wait_ge(self.PE.h, self.PE.n)
            ws = self.dma("sy", dslab[:].rearrange("p (a b) -> p a b", b=512),
                          self.d_dT[ob].rearrange("(a p) b -> p a b", p=128))
            self.te.wait_ge(self.SY.h, ws)
            dsl3 = dslab[:].rearrange("p (a b) -> p a b", b=512)
            for tt in range(8):
                row = sh * 8 + tt
                bk = tt % 4
                pbk = self.pb[bk]
                xs = tt % 2
                self.gp.wait_ge(self.VE.h, dn_evac.get((xs,), 0))
                self.gp.wait_ge(self.SY.h, self.SY.n)
                wx = self.dma("gp", x1t[:, xs * 512:(xs + 1) * 512],
                              self.i_xo[row * 128:(row + 1) * 128,
                                        ob * 512:(ob + 1) * 512])
                self.bank_wait(self.te, bk)
                for dc in range(32):
                    self.t(lambda pbk=pbk, dc=dc, tt=tt, dsl3=dsl3: self.te.matmul(
                        pbk, ctx[:, dc * 1024 + tt * 128: dc * 1024 + tt * 128 + 128],
                        dsl3[:, dc, :], start=(dc == 0), stop=(dc == 31)))
                pe_n = self.PE.n
                self.ve.wait_ge(self.PE.h, pe_n)
                self.ve.wait_ge(self.GP.h, wx)
                self.v(lambda pbk=pbk, xs=xs: self.ve.tensor_tensor(
                    out=x1t[:, xs * 512:(xs + 1) * 512], in0=pbk,
                    in1=x1t[:, xs * 512:(xs + 1) * 512], op=ALU.add))
                nev = self.v(lambda row=row, ob=ob, xs=xs: self.ve.bn_stats(
                    self.statsB[:, row * 48 + ob * 6: row * 48 + (ob + 1) * 6],
                    x1t[:, xs * 512:(xs + 1) * 512]))
                self.bank_used(bk, "VE", nev)
                dn_evac[(xs,)] = nev
                self.sy.wait_ge(self.VE.h, nev)
                self.dma("sy", self.d_x1[row * 128:(row + 1) * 128,
                                         ob * 512:(ob + 1) * 512],
                         x1t[:, xs * 512:(xs + 1) * 512])

    def p45(self):
        nc = self.nc
        for th in range(2):
            with nc.sbuf_tensor(f"lnfm2_{th}", [128, 32 * 1024], BF16) as lnfm2:
                self._p4_half(th, lnfm2)
                self.bar()
                self._p5_half(th, lnfm2)
                self.bar()

    def _p4_half(self, th, lnfm2):
        nc = self.nc
        with nc.sbuf_tensor(f"x1b_{th}", [128, 2 * H], F32) as x1b, \
             nc.sbuf_tensor(f"ln2tm_{th}", [128, 2 * H], BF16) as ln2tm, \
             nc.sbuf_tensor(f"xgt_{th}", [128, 32], F32) as xgt, \
             nc.sbuf_tensor(f"xgtb_{th}", [128, 32], BF16) as xgtb:
            x1_free = [0, 0]
            ln2_free = [0, 0]
            for tt in range(8):
                row = th * 8 + tt
                xs = tt % 2
                self.gp.wait_ge(self.VE.h, x1_free[xs])
                wx = self.dma("gp", x1b[:, xs * H:(xs + 1) * H],
                              self.d_x1[row * 128:(row + 1) * 128, :])
                self.v(lambda row=row: self.ve.bn_aggr(
                    self.smA[:, 0:2], self.statsB[:, row * 48:(row + 1) * 48]))
                self.v(lambda: self.ve.tensor_scalar(
                    out=self.smA[:, 2:3], in0=self.smA[:, 1:2], scalar1=EPS,
                    scalar2=None, op0=ALU.add))
                self.sc.wait_ge(self.VE.h, self.VE.n)
                self.s(lambda: self.sc.activation(
                    out=self.smA[:, 3:4], in_=self.smA[:, 2:3], func=AF.Sqrt))
                self.ve.wait_ge(self.AC.h, self.AC.n)
                self.v(lambda: self.ve.reciprocal(self.smA[:, 4:5], self.smA[:, 3:4]))
                self.ve.wait_ge(self.GP.h, wx)
                self.ve.wait_ge(self.PE.h, ln2_free[xs])
                nl = self.v(lambda xs=xs: self.ve.tensor_scalar(
                    out=ln2tm[:, xs * H:(xs + 1) * H], in0=x1b[:, xs * H:(xs + 1) * H],
                    scalar1=self.smA[:, 0:1], scalar2=self.smA[:, 4:5],
                    op0=ALU.subtract, op1=ALU.mult))
                x1_free[xs] = nl
                nxg = self.v(lambda xs=xs: self.ve.tensor_reduce(
                    xgt[:, 0:32],
                    ln2tm[:, xs * H:(xs + 1) * H].rearrange("p (a b) -> p a b", b=128),
                    AX.X, ALU.add))
                nxg = self.v(lambda: self.ve.tensor_copy(xgtb[:, 0:32], xgt[:, 0:32]))
                self.te.wait_ge(self.VE.h, nxg)
                self.bank_wait(self.te, 6)
                ins = self.te.transpose(self.pbh[6][0:32, 0:128],
                                        xgtb[:, 0:32], self.ident[:])
                pe_n = self.PE.inc(ins)
                self.ve.wait_ge(self.PE.h, pe_n)
                n = self.v(lambda row=row: self.ve.tensor_copy(
                    self.Xg[0:32, row * 128 % 2048:(row * 128 % 2048) + 128],
                    self.pbh[6][0:32, 0:128]))
                self.bank_used(6, "VE", n)
                lnfm23 = lnfm2[:].rearrange("p (hc t) -> p hc t", t=1024)
                for hg in range(8):
                    bk = 6 + (hg % 2)
                    self.bank_wait(self.te, bk)
                    for hj in range(4):
                        hc = hg * 4 + hj
                        tq, jq = hc // 8, hc % 8
                        ins = self.te.transpose(
                            self.pbh[bk][:, hj * 128:(hj + 1) * 128],
                            ln2tm[:, xs * H + tq * 1024 + jq: xs * H + (tq + 1) * 1024: 8],
                            self.ident[:])
                    pe_n = self.PE.inc(ins)
                    self.ve.wait_ge(self.PE.h, pe_n)
                    n = self.v(lambda bk=bk, hg=hg, tt=tt, lnfm23=lnfm23:
                               self.ve.tensor_copy(
                        lnfm23[:, hg * 4:(hg + 1) * 4, tt * 128:(tt + 1) * 128],
                        self.pbh[bk][:, 0:512].rearrange("p (a b) -> p a b", b=128)))
                    self.bank_used(bk, "VE", n)
                ln2_free[xs] = self.PE.n

    def _p5_half(self, th, lnfm2):
        nc = self.nc
        with nc.sbuf_tensor(f"qrb_{th}", [128, 2 * 512], I32) as qrb, \
             nc.sbuf_tensor(f"qmb_{th}", [128, 4096], I32) as qmb, \
             nc.sbuf_tensor(f"srb_{th}", [128, 2 * 512], BF16) as srb, \
             nc.sbuf_tensor(f"wgb_{th}", [128, 2 * 4096], BF16) as wgb, \
             nc.sbuf_tensor(f"wub_{th}", [128, 2 * 4096], BF16) as wub, \
             nc.sbuf_tensor(f"nzb_{th}", [32, 2 * 256], BF16) as nzb, \
             nc.sbuf_tensor(f"silu_{th}", [128, 512], BF16) as silu, \
             nc.sbuf_tensor(f"intb_{th}", [128, 2 * 1024], BF16) as intb:
            gq3 = self.i_gq.rearrange("(c m) i -> m c i", m=16)
            uq3 = self.i_uq.rearrange("(c m) i -> m c i", m=16)
            st = {"qr": {}, "w": {}, "int": {}}
            self.gp.wait_ge(self.SC.h, self.qtile_done)
            self.gp.wait_ge(self.SC.h, self.qtile_done2)
            for ic in range(NCI):
                self._p5_ic(th, ic, lnfm2, qrb, qmb, srb, wgb, wub, nzb, silu,
                            intb, gq3, uq3, st)

    def _p5_ic(self, th, ic, lnfm2, qrb, qmb, srb, wgb, wub, nzb, silu, intb,
               gq3, uq3, st):
        wsl = ic % 2
        par = 0 if ic % 2 == 0 else 4
        mats = ((0, self.d_gqc, self.d_sgr, wgb), (1, self.d_uqc, self.d_sur, wub))
        half_off = wsl * 4096
        hs = wsl * 512
        for mi, q3, dsc, wdst in mats:
            self.gp.wait_ge(self.VE.h, st["qr"].get(wsl, 0))
            self.dma("gp", qrb[:, hs:hs + 512], q3[ic, :, :])
            self.gp.wait_ge(self.VE.h, st.setdefault("sr", {}).get(wsl, 0))
            w2 = self.dma("gp", srb[:, hs:hs + 512], dsc[ic, :, :])
            st.setdefault("nz", {})[mi] = self.dma(
                "sy", nzb[:, wsl * 256 + mi * 128: wsl * 256 + (mi + 1) * 128],
                self.d_nzp[mi, :, ic * 128:(ic + 1) * 128])
            self.ve.wait_ge(self.GP.h, w2)
            for j in range(8):
                self.v(lambda j=j, hs=hs: self.ve.tensor_scalar(
                    out=qmb[:, j * 512:(j + 1) * 512], in0=qrb[:, hs:hs + 512],
                    scalar1=j * 4, scalar2=15,
                    op0=ALU.logical_shift_right, op1=ALU.bitwise_and))
            st["qr"][wsl] = self.VE.n
            self.ve.wait_ge(self.PE.h, st["w"].get((wsl, mi), 0))
            nw = self.v(lambda wdst=wdst, half_off=half_off, hs=hs:
                        self.ve.tensor_tensor(
                out=wdst[:, half_off:half_off + 4096]
                .rearrange("p (t j i) -> p t j i", t=4, i=128),
                in0=qmb[:].rearrange("p (j t i) -> p t j i", j=8, i=128),
                in1=srb[:, hs:hs + 512].rearrange("p (t i) -> p t i", i=128)
                [:, :, None, :].broadcast_to([128, 4, 8, 128]),
                op=ALU.mult))
            st["sr"][wsl] = nw
        self.te.wait_ge(self.VE.h, nw)
        self.te.wait_ge(self.SY.h, st["nz"][1])
        for mi, _, _, wsrc in mats:
            for tc in range(2):
                bk = mi * 2 + tc + par
                self.bank_wait(self.te, bk)
                pbk = self.pb[bk]
                self.t(lambda pbk=pbk, mi=mi, tc=tc, wsl=wsl: self.te.matmul(
                    pbk, nzb[:, wsl * 256 + mi * 128: wsl * 256 + (mi + 1) * 128],
                    self.Xg[0:32, th * 1024 + tc * 512: th * 1024 + (tc + 1) * 512],
                    start=True, stop=False))
                for c in range(32):
                    self.t(lambda pbk=pbk, c=c, tc=tc, wsrc=wsrc, half_off=half_off:
                           self.te.matmul(
                        pbk, wsrc[:, half_off + c * 128: half_off + (c + 1) * 128],
                        lnfm2[:, c * 1024 + tc * 512: c * 1024 + (tc + 1) * 512],
                        start=False, stop=(c == 31)))
        st["w"][(wsl, 0)] = self.PE.n
        st["w"][(wsl, 1)] = self.PE.n
        st.setdefault("pe_mm", {})[ic] = self.PE.n
        self._p5_evac(th, ic - 1, silu, intb, st)
        if ic == NCI - 1:
            self._p5_evac(th, ic, silu, intb, st)

    def _p5_evac(self, th, ic, silu, intb, st):
        if ic < 0:
            return
        wsl = ic % 2
        par = 0 if ic % 2 == 0 else 4
        pe_mm = st["pe_mm"][ic]
        for tc in range(2):
            bkg = 0 + tc + par
            bku = 2 + tc + par
            self.sc.wait_ge(self.PE.h, pe_mm)
            self.sc.wait_ge(self.VE.h, st.get("lastint", 0))
            nsil = self.s(lambda bkg=bkg: self.sc.activation(
                out=silu[:], in_=self.pb[bkg], func=AF.Silu))
            self.ve.wait_ge(self.AC.h, nsil)
            self.ve.wait_ge(self.SY.h, st["int"].get((wsl, tc), 0))
            nint = self.v(lambda bku=bku, tc=tc, wsl=wsl: self.ve.tensor_tensor(
                out=intb[:, wsl * 1024 + tc * 512: wsl * 1024 + (tc + 1) * 512],
                in0=silu[:], in1=self.pb[bku], op=ALU.mult))
            self.bank_used(bkg, "AC", nsil)
            self.bank_used(bku, "VE", nint)
            st["lastint"] = nint
            self.sy.wait_ge(self.VE.h, nint)
            st["int"][(wsl, tc)] = self.dma(
                "sy", self.d_int[ic, :, th * 1024 + tc * 512:
                                 th * 1024 + (tc + 1) * 512],
                intb[:, wsl * 1024 + tc * 512: wsl * 1024 + (tc + 1) * 512])

    def p6(self):
        nc = self.nc
        with nc.sbuf_tensor("intf", [128, NCI * 512], BF16) as intf, \
             nc.sbuf_tensor("wdt", [128, 2 * 8 * 512], BF16) as wdt, \
             nc.sbuf_tensor("f16t", [128, 512], F16) as f16t, \
             nc.sbuf_tensor("x1r", [128, 2 * 512], F32) as x1r, \
             nc.sbuf_tensor("yot", [128, 2 * 512], F32) as yot:
            for tc in range(4):
                wi = 0
                self.sy.wait_ge(self.PE.h, self.PE.n)
                for ic in range(NCI):
                    wi = self.dma("sy", intf[:, ic * 512:(ic + 1) * 512],
                                  self.d_int[ic, :, tc * 512:(tc + 1) * 512])
                self.te.wait_ge(self.SY.h, wi)
                wd_free = {}
                ev_done = {}
                for hb in range(8):
                    par = hb % 2
                    nblk = (NCI + 7) // 8
                    for blk in range(nblk):
                        ic0 = blk * 8
                        nic = min(8, NCI - ic0)
                        dsl = blk % 2
                        self.gp.wait_ge(self.PE.h, wd_free.get(dsl, 0))
                        wwd = self.dma(
                            "gp",
                            wdt[:, dsl * 4096: dsl * 4096 + nic * 512]
                            .rearrange("p (a b) -> p a b", b=512),
                            self.d_wd[ic0:ic0 + nic, :, hb * 512:(hb + 1) * 512]
                            .rearrange("a p b -> p a b"))
                        self.te.wait_ge(self.GP.h, wwd)
                        for ici in range(nic):
                            ic = ic0 + ici
                            for tt in range(4):
                                bk = par * 4 + tt
                                if ic == 0:
                                    self.bank_wait(self.te, bk)
                                self.t(lambda bk=bk, ic=ic, tt=tt, dsl=dsl, ici=ici:
                                       self.te.matmul(
                                    self.pb[bk],
                                    intf[:, ic * 512 + tt * 128: ic * 512 + tt * 128 + 128],
                                    wdt[:, dsl * 4096 + ici * 512: dsl * 4096 + (ici + 1) * 512],
                                    start=(ic == 0), stop=(ic == NCI - 1)))
                        wd_free[dsl] = self.PE.n
                    pe_mm = self.PE.n
                    for tt in range(4):
                        bk = par * 4 + tt
                        row = tc * 4 + tt
                        xs = tt % 2
                        self.gp.wait_ge(self.VE.h, ev_done.get(xs, 0))
                        wx = self.dma("gp", x1r[:, xs * 512:(xs + 1) * 512],
                                      self.d_x1[row * 128:(row + 1) * 128,
                                                hb * 512:(hb + 1) * 512])
                        self.ve.wait_ge(self.PE.h, pe_mm)
                        self.ve.wait_ge(self.GP.h, wx)
                        self.ve.wait_ge(self.SY.h, self.SY.n)
                        self.v(lambda bk=bk: self.ve.tensor_copy(f16t[:], self.pb[bk]))
                        nev = self.v(lambda xs=xs: self.ve.tensor_tensor(
                            out=yot[:, xs * 512:(xs + 1) * 512], in0=f16t[:],
                            in1=x1r[:, xs * 512:(xs + 1) * 512], op=ALU.add))
                        self.bank_used(bk, "VE", nev)
                        ev_done[xs] = nev
                        self.sy.wait_ge(self.VE.h, nev)
                        self.dma("sy", self.o_y[row * 128:(row + 1) * 128,
                                                hb * 512:(hb + 1) * 512],
                                 yot[:, xs * 512:(xs + 1) * 512])

    def build(self, upto=None):
        with self.nc.named_scope("ph0a"):
            self.p0_consts()
            self.p0_qrep()
            self.p0_scales()
            self.p0_zeros()
            self.bar()
        with self.nc.named_scope("ph0wd"):
            self.p0_wd()
            self.bar()
        if upto == "p0":
            return self.finish()
        with self.nc.named_scope("ph1"):
            self.p1()
            self.bar()
        if upto == "p1":
            return self.finish()
        with self.nc.named_scope("ph3"):
            self.p3()
            self.bar()
        if upto == "p3":
            return self.finish()
        with self.nc.named_scope("ph45"):
            self.p45()
            self.bar()
        if upto == "p5":
            return self.finish()
        with self.nc.named_scope("ph6"):
            self.p6()
        return self.finish()

    def finish(self):
        self.bar()
        self.sy.wait_ge(self.SY.h, self.SY.n)
        self.gp.wait_ge(self.GP.h, self.GP.n)
        self.sc.wait_ge(self.SC.h, self.SC.n)
        return self.nc


def shard_inputs(inputs):
    cos_g, sin_g, ident, shifts, masks = host_tables()
    x = np.ascontiguousarray(inputs["L0_attn_hidden_states"])
    qkv_w = np.ascontiguousarray(inputs["L0_attn_qkv_w"])
    qkv_b = np.ascontiguousarray(inputs["L0_attn_qkv_b"]).reshape(1, 4608)
    dense_w = np.ascontiguousarray(inputs["L0_attn_dense_w"])
    in_maps = []
    for core in range(8):
        b, half = core // 2, core % 2
        rows = np.concatenate([np.arange(g * CH, (g + 1) * CH)
                               for g in SLOTS[half]])
        im = dict(
            x_glob=np.ascontiguousarray(x[b]),
            x_own=np.ascontiguousarray(x[b][rows]),
            cos_g=cos_g, sin_g=sin_g,
            cos_o=np.ascontiguousarray(cos_g[rows]),
            sin_o=np.ascontiguousarray(sin_g[rows]),
            masks=masks[half], ident=ident, shifts=shifts,
            qkv_w=qkv_w, qkv_b=qkv_b, dense_w=dense_w,
            gate_q=np.ascontiguousarray(inputs["L0_ffn_gate_qweight"]),
            gate_s=np.ascontiguousarray(inputs["L0_ffn_gate_scales"]),
            gate_z=np.ascontiguousarray(inputs["L0_ffn_gate_zeros"]),
            up_q=np.ascontiguousarray(inputs["L0_ffn_up_qweight"]),
            up_s=np.ascontiguousarray(inputs["L0_ffn_up_scales"]),
            up_z=np.ascontiguousarray(inputs["L0_ffn_up_zeros"]),
            down_q=np.ascontiguousarray(inputs["L0_ffn_down_qweight"]),
            down_s=np.ascontiguousarray(inputs["L0_ffn_down_scales"]),
            down_z=np.ascontiguousarray(inputs["L0_ffn_down_zeros"]),
        )
        in_maps.append(im)
    return in_maps


def kernel(**inputs):
    in_maps = shard_inputs(inputs)
    kb = KB()
    nc = kb.build()
    res = run_bass_kernel_spmd(nc, in_maps, core_ids=list(range(8)))
    out = np.zeros((B, S, H), dtype=np.float32)
    for core in range(8):
        b, half = core // 2, core % 2
        y = res.results[core]["y_own"]
        for si, g in enumerate(SLOTS[half]):
            out[b, g * CH:(g + 1) * CH] = y[si * CH:(si + 1) * CH]
    return out
